# version 02
# baseline (speedup 1.0000x reference)
import math
import numpy as np
import ml_dtypes

import concourse.bass as bass
import concourse.mybir as mybir
from concourse.bass_utils import run_bass_kernel_spmd

bfdt = ml_dtypes.bfloat16
F32 = mybir.dt.float32
BF16 = mybir.dt.bfloat16
AF = mybir.ActivationFunctionType
ALU = mybir.AluOpType
AX = mybir.AxisListType

MAGIC = float(np.float32(1.5 * 2 ** 23))
D = 2048
T = 1024
NF = 16
I8 = 8192
SEQ = 4096
INV_SQRT128 = 1.0 / math.sqrt(128.0)
INV_SQRT32 = 1.0 / math.sqrt(32.0)

PHASES_DEFAULT = 4
PARANOID = [True]
PARANOID_SEL = [None]
DRAIN_INC = [False]


ENGINES = ("sp", "pe", "act", "dve", "pool")


class Buf:
    __slots__ = ("name", "last_writer", "readers")

    def __init__(self, name):
        self.name = name
        self.last_writer = None
        self.readers = []


class Task:
    __slots__ = ("eng", "emit", "inc", "deps", "cum", "pre", "raw_inc",
                 "rd", "wr")


class Sched:
    def __init__(self):
        self.tasks = []
        self.latest = {e: None for e in ENGINES}

    def buf(self, name):
        return Buf(name)

    def task(self, eng, emit, reads=(), writes=(), inc=1, after=(),
             raw_inc=False):
        assert eng in ENGINES, eng
        t = Task()
        t.eng, t.emit, t.inc = eng, emit, inc
        t.raw_inc = raw_inc
        t.rd, t.wr = list(reads), list(writes)
        deps = set()
        for b in reads:
            if b.last_writer is not None:
                deps.add(b.last_writer)
        war_engines = set()
        for b in writes:
            if b.last_writer is not None:
                deps.add(b.last_writer)
            for r in b.readers:
                deps.add(r)
                if r.eng != eng:
                    war_engines.add(r.eng)
        for e2 in war_engines:
            if self.latest[e2] is not None:
                deps.add(self.latest[e2])
        for d in after:
            if d is not None:
                deps.add(d)
        t.deps = [d for d in deps if d is not t]
        self.tasks.append(t)
        self.latest[eng] = t
        for b in reads:
            b.readers.append(t)
        for b in writes:
            b.last_writer = t
            b.readers = []
        return t

    def build_bodies(self, nc, paranoid=False):
        sems = {e: nc.alloc_semaphore("sem_" + e) for e in ENGINES}
        cum = {e: 0 for e in ENGINES}
        for t in self.tasks:
            t.pre = dict(cum)
            cum[t.eng] += t.inc
            t.cum = cum[t.eng]
        total = dict(cum)
        streams = {e: [t for t in self.tasks if t.eng == e] for e in ENGINES}
        if paranoid:
            class _D:
                pass
            sel = PARANOID_SEL[0]
            for ti, t in enumerate(self.tasks):
                if sel is not None and not sel(ti, t):
                    continue
                nd = []
                for e in ENGINES:
                    if e != t.eng and t.pre[e] > 0:
                        d = _D(); d.eng = e; d.cum = t.pre[e]
                        nd.append(d)
                keep = [d for d in t.deps if d.eng == t.eng]
                t.deps = nd + keep

        def body_for(ename, _par=paranoid):
            hazard = ename in ("dve", "act", "pool")
            def body(eng):
                seen = {e: 0 for e in ENGINES}
                for t in streams[ename]:
                    need = {}
                    same = False
                    for d in t.deps:
                        if d.eng != ename:
                            need[d.eng] = max(need.get(d.eng, 0), d.cum)
                        else:
                            same = True
                    for e2 in sorted(need):
                        if seen[e2] < need[e2]:
                            eng.wait_ge(sems[e2], need[e2])
                            seen[e2] = need[e2]
                    if hazard and (same or _par):
                        eng.drain()
                    insts = t.emit(eng)
                    if isinstance(insts, (list, tuple)) and t.inc == 16 * len(insts):
                        for i2 in insts:
                            i2.then_inc(sems[t.eng], 16)
                    else:
                        last = (insts[-1] if isinstance(insts, (list, tuple))
                                else insts)
                        if DRAIN_INC[0] and hazard and t.inc == 1 and not t.raw_inc:
                            last = eng.drain()
                        last.then_inc(sems[t.eng], t.inc)
                if ename == "sp":
                    for e2 in ENGINES:
                        eng.wait_ge(sems[e2], total[e2])
            return body
        return body_for



def build_nc(phases=PHASES_DEFAULT):
    nc = bass.Bass("TRN2", target_bir_lowering=False)
    s = Sched()

    def di(name, shape, dt=F32):
        return nc.dram_tensor(name, shape, dt, kind="ExternalInput")

    xin = di("xin", [D, T])
    wqkv = di("wqkv", [D, 6144])
    woutp = di("woutp", [D, D])
    wq_b = di("wq_b", [D, D]); wk_b = di("wk_b", [D, D])
    wv_b = di("wv_b", [D, D]); wo_b = di("wo_b", [D, D])
    wgT = di("wgT", [D, I8]); wuT = di("wuT", [D, I8]); wdT = di("wdT", [I8, D])
    biascols = di("biascols", [128, 256])
    wgS = di("wgS", [256, I8]); wuS = di("wuS", [256, I8])
    wdS = di("wdS", [1024, D])
    bvrow = di("bvrow", [1, D], BF16)
    consts = di("consts", [128, 512])
    constb = di("constb", [128, 256], BF16)

    og = nc.dram_tensor("og", [D, T], F32, kind="ExternalOutput")
    dbg = nc.dram_tensor("dbg", [128, 8 * T], F32, kind="ExternalOutput")

    scr_rs = nc.dram_tensor("scr_rs", [16, T], F32, kind="Internal")
    scr_h = nc.dram_tensor("scr_h", [I8, T], F32, kind="Internal")
    scr_qg = nc.dram_tensor("scr_qg", [D, I8], BF16, kind="Internal")
    scr_qu = nc.dram_tensor("scr_qu", [D, I8], BF16, kind="Internal")
    scr_qd = nc.dram_tensor("scr_qd", [I8, D], BF16, kind="Internal")
    scr_x = nc.dram_tensor("scr_x", [D, T], F32, kind="Internal")
    cc_kin = nc.dram_tensor("cc_kin", [D, T], BF16, kind="Internal")
    cc_kout = nc.dram_tensor("cc_kout", [4 * D, T], BF16, kind="Internal")
    cc_vin = nc.dram_tensor("cc_vin", [T, D], BF16, kind="Internal")
    cc_vout = nc.dram_tensor("cc_vout", [SEQ, D], BF16, kind="Internal")
    cc_min = nc.dram_tensor("cc_min", [1, 4], F32, kind="Internal")
    cc_mout = nc.dram_tensor("cc_mout", [1, 4], F32, kind="Internal")

    def at(name, shape, dt, off):
        return nc.alloc_sbuf_tensor_at(name, shape, dt, offset=off)

    B0 = 16384
    cons = at("cons", [128, 512], F32, B0 + 0)
    consbt = at("consbt", [128, 256], BF16, B0 + 2048)
    tmr = at("tmr", [128, 128], F32, B0 + 2560)
    bias = at("bias", [128, 256], F32, B0 + 3072)
    rows = at("rows", [128, 1024], F32, B0 + 4096)
    fsc = at("fsc", [128, 3072], F32, B0 + 8192)
    gsc = at("gsc", [128, 1536], F32, B0 + 20480)
    wstr = at("wstr", [128, 8192], BF16, B0 + 26624)
    regA = at("regA", [128, 16384], BF16, B0 + 43008)
    regB = at("regB", [128, 16384], BF16, B0 + 75776)
    regC = at("regC", [128, 16384], BF16, B0 + 108544)
    x = at("x", [128, 16384], F32, B0 + 141312)
    qstage = at("qstage", [128, 2944], BF16, B0 + 206848)
    qh = at("qh", [128, 65536], BF16, B0 + 43008)
    END = B0 + 212736
    assert END <= 229376 - 32

    xln = regA
    qa = regB
    ka = regC

    bx = [s.buf("x%d" % c) for c in range(NF)]
    bxln = [s.buf("xln%d" % c) for c in range(NF)]
    bqa = s.buf("qa")
    bka = s.buf("ka")
    bwstr = [s.buf("wstr%d" % i) for i in range(4)]
    bfsc = s.buf("fsc")
    bgsc = s.buf("gsc")
    bcons = s.buf("cons")
    btmr = s.buf("tmr")
    brow = s.buf("rows")
    bbias = s.buf("bias")
    bps = [s.buf("ps%d" % i) for i in range(8)]
    bvs = s.buf("vsum")
    bqx = s.buf("qx")
    bqb = s.buf("q_b")
    bfe = s.buf("feats")
    bscr = {k: s.buf("scr_" + k) for k in
            ("rs", "h", "qg", "qu", "qd", "x", "kin", "kout", "vin", "vout",
             "min", "mout")}

    PD = [nc.alloc_psum_tensor("pd%d" % i, [128, 1024], F32) for i in range(4)]
    PS = [PD[i // 2][:, (i % 2) * 512:(i % 2) * 512 + 512] for i in range(8)]

    amask = cons[:, 0:256]
    ident_f = cons[:, 256:384]
    ones_f = cons[:, 384:512]
    ident_b = consbt[:, 0:128]
    ones_b = consbt[:, 128:256]

    for c in range(NF):
        s.task("sp", lambda e, c=c: e.dma_start(
            x[:, c * T:(c + 1) * T], xin[c * 128:(c + 1) * 128, :]),
            writes=[bx[c]], inc=16)
    s.task("sp", lambda e: e.dma_start(cons[:], consts[:]), writes=[bcons], inc=16)
    s.task("sp", lambda e: e.dma_start(consbt[:], constb[:]), writes=[bcons], inc=16)
    s.task("sp", lambda e: e.dma_start(bias[:], biascols[:]), writes=[bbias], inc=16)

    def row_to_tm(row_ap, tm_col0, nblk, reads, psi):
        for i in range(nblk):
            s.task("pe", lambda te, i=i: te.matmul(
                PS[psi][:, 64 + i:65 + i],
                row_ap[0:1, i * 128:(i + 1) * 128], ones_f[0:1, 0:1]),
                reads=list(reads) + [bcons], writes=[bps[psi]])
        s.task("dve", lambda v: v.tensor_copy(
            tmr[:, tm_col0:tm_col0 + nblk], PS[psi][:, 64:64 + nblk]),
            reads=[bps[psi]], writes=[btmr])

    def tm_to_row(tm_col0, row_off, nblk, psi):
        for i in range(nblk):
            s.task("pe", lambda te, i=i: te.matmul(
                PS[psi][0:1, 0:128],
                tmr[:, tm_col0 + i:tm_col0 + i + 1], ident_f[:, 0:128]),
                reads=[btmr, bcons], writes=[bps[psi]])
            s.task("dve", lambda v, i=i: v.tensor_copy(
                rows[0:1, row_off + i * 128:row_off + (i + 1) * 128],
                PS[psi][0:1, 0:128]),
                reads=[bps[psi]], writes=[brow])

    def bcast_row(row_off, dst_ap, dst_buf, n, psi):
        for h0 in range(0, n, 512):
            w = min(512, n - h0)
            s.task("pe", lambda te, h0=h0, w=w: te.matmul(
                PS[psi][:, 0:w], ones_f[0:1, 0:128],
                rows[0:1, row_off + h0:row_off + h0 + w]),
                reads=[brow, bcons], writes=[bps[psi]])
            s.task("dve", lambda v, h0=h0, w=w: v.tensor_copy(
                dst_ap[:, h0:h0 + w], PS[psi][:, 0:w]),
                reads=[bps[psi]], writes=[dst_buf])

    def layernorm(emit_xln=True):
        for (psi, use_sq, rowoff) in ((6, False, 0), (7, True, 0)):
            for half in range(2):
                t0 = half * 512
                for c in range(NF):
                    if use_sq:
                        s.task("act", lambda a, c=c, t0=t0: a.activation(
                            gsc[:, 0:512], x[:, c * T + t0:c * T + t0 + 512],
                            AF.Square),
                            reads=[bx[c]], writes=[bgsc])
                        mv, mb = gsc[:, 0:512], bgsc
                    else:
                        mv, mb = x[:, c * T + t0:c * T + t0 + 512], bx[c]
                    s.task("pe", lambda te, c=c, mv=mv, psi=psi: te.matmul(
                        PS[psi][0:1, 0:512], ones_f[:, 0:1], mv,
                        start=(c == 0), stop=(c == NF - 1)),
                        reads=[mb, bcons], writes=[bps[psi]])
                s.task("dve", lambda v, t0=t0, psi=psi: v.tensor_copy(
                    rows[0:1, t0:t0 + 512], PS[psi][0:1, 0:512]),
                    reads=[bps[psi]], writes=[brow])
            row_to_tm(rows, 0 if not use_sq else 8, 8, [brow], psi)
        def stats(v):
            m = tmr[:, 16:24]; ex2 = tmr[:, 24:32]
            msq = tmr[:, 32:40]; var = tmr[:, 40:48]
            v.tensor_scalar_mul(m, tmr[:, 0:8], 1.0 / D)
            v.tensor_scalar(ex2, tmr[:, 8:16], 1.0 / D, 1e-5, ALU.mult, ALU.add)
            v.scalar_tensor_tensor(msq, tmr[:, 0:8], 1.0 / D, tmr[:, 0:8],
                                   ALU.mult, ALU.mult)
            v.drain()
            return v.scalar_tensor_tensor(var, msq, -1.0 / D, ex2,
                                          ALU.mult, ALU.add)
        s.task("dve", stats, reads=[btmr], writes=[btmr])
        s.task("act", lambda a: a.activation(
            tmr[:, 48:56], tmr[:, 40:48], AF.Ln), reads=[btmr], writes=[btmr])
        s.task("act", lambda a: a.activation(
            tmr[:, 56:64], tmr[:, 48:56], AF.Exp, scale=-0.5),
            reads=[btmr], writes=[btmr])
        s.task("dve", lambda v: v.scalar_tensor_tensor(
            tmr[:, 64:72], tmr[:, 16:24], -1.0, tmr[:, 56:64],
            ALU.mult, ALU.mult),
            reads=[btmr], writes=[btmr])
        if not emit_xln:
            return
        tm_to_row(56, 0, 8, 6)
        bcast_row(0, fsc[:, 0:T], bfsc, T, 6)
        tm_to_row(64, 0, 8, 7)
        bcast_row(0, fsc[:, T:2 * T], bfsc, T, 7)
        for c in range(NF):
            s.task("dve", lambda v, c=c: v.tensor_mul(
                gsc[:, 0:T], x[:, c * T:(c + 1) * T], fsc[:, 0:T]),
                reads=[bx[c], bfsc], writes=[bgsc])
            s.task("dve", lambda v, c=c: v.tensor_add(
                xln[:, c * T:(c + 1) * T], gsc[:, 0:T], fsc[:, T:2 * T]),
                reads=[bgsc, bfsc], writes=[bxln[c]])

    def proj_stream(w_dram, din, dout, moving, out_cb, oc_grp=2, cast=True,
                    nmov=512, wdt=None):
        nfc = din // 128
        gw = 128 * oc_grp
        ngrp = dout // gw
        nth = T // nmov
        assert nfc * gw <= 4096, "weight group exceeds stream buffer"
        for g in range(ngrp):
            wb = g % 2
            def ld(e, g=g, wb=wb):
                src = w_dram[:, g * gw:(g + 1) * gw].rearrange(
                    "(fc p) o -> p fc o", p=128)
                dst = wstr[:, wb * 4096:wb * 4096 + nfc * gw].rearrange(
                    "p (fc o) -> p fc o", o=gw)
                return e.dma_start(dst, src)
            s.task("pool" if cast else "sp", ld, writes=[bwstr[wb]], inc=16)
            for oci in range(oc_grp):
                oc = g * oc_grp + oci
                for th in range(nth):
                    psi = (oc * nth + th) % 4
                    for fc in range(nfc):
                        mv, mbuf = moving(fc)
                        s.task("pe", lambda te, fc=fc, oci=oci, th=th, psi=psi,
                               wb=wb, mv=mv, nfc=nfc, gw=gw: te.matmul(
                                   PS[psi][:, 0:nmov],
                                   wstr[:, wb * 4096 + fc * gw + oci * 128:
                                        wb * 4096 + fc * gw + oci * 128 + 128],
                                   mv[:, th * nmov:(th + 1) * nmov],
                                   start=(fc == 0), stop=(fc == nfc - 1)),
                               reads=[bwstr[wb], mbuf], writes=[bps[psi]])
                    out_cb(oc, th, PS[psi][:, 0:nmov], bps[psi])

    layernorm()

    def qkv_out(oc, th, ps, psb):
        h, kind = divmod(oc, 3)
        t0 = th * 512
        if kind == 0:
            s.task("dve", lambda v: v.tensor_scalar_add(
                qa[:, h * T + t0:h * T + t0 + 512], ps, bias[:, oc:oc + 1]),
                reads=[psb, bbias], writes=[bqa])
        elif kind == 1:
            s.task("dve", lambda v: v.tensor_scalar_add(
                ka[:, h * T + t0:h * T + t0 + 512], ps, bias[:, oc:oc + 1]),
                reads=[psb, bbias], writes=[bka])
        else:
            if h == 0:
                s.task("dve", lambda v: v.tensor_scalar_add(
                    fsc[:, 2 * T + t0:2 * T + t0 + 512], ps,
                    bias[:, oc:oc + 1]),
                    reads=[psb, bbias], writes=[bvs])
            else:
                s.task("dve", lambda v: v.scalar_tensor_tensor(
                    fsc[:, 2 * T + t0:2 * T + t0 + 512], ps,
                    bias[:, oc:oc + 1], fsc[:, 2 * T + t0:2 * T + t0 + 512],
                    ALU.add, ALU.add),
                    reads=[psb, bbias, bvs], writes=[bvs])

    proj_stream(wqkv, D, 6144,
                lambda fc: (xln[:, fc * T:(fc + 1) * T], bxln[fc]), qkv_out)

    bdbg = s.buf("dbg")
    s.task("sp", lambda e: e.dma_start(dbg[:, 0 * T:1 * T].bitcast(F32), fsc[:, 0:T]),
           reads=[bfsc], writes=[bdbg], inc=16)
    s.task("pool", lambda e: e.dma_start(dbg[:, 1 * T:2 * T], xln[:, 0:T]),
           reads=[bxln[0]], writes=[bdbg], inc=16)
    s.task("pool", lambda e: e.dma_start(dbg[:, 2 * T:3 * T], qa[:, 0:T]),
           reads=[bqa], writes=[bdbg], inc=16)
    s.task("sp", lambda e: e.dma_start(dbg[:, 3 * T:4 * T], fsc[:, 2 * T:3 * T]),
           reads=[bvs], writes=[bdbg], inc=16)

    s.task("sp", lambda e: e.dma_start(dbg[0:1, 6 * T:7 * T], rows[0:1, 0:T]),
           reads=[brow], writes=[bdbg], inc=16)
    s.task("sp", lambda e: e.dma_start(dbg[:, 7 * T:7 * T + 128], tmr[:, 0:128]),
           reads=[btmr], writes=[bdbg], inc=16)
    s.task("dve", lambda v: v.memset(PS[4][:, 0:256], 0.0), writes=[bps[4]])
    brs = s.buf("rs_all")
    for batch in range(16):
        def sc(te, batch=batch):
            out = []
            for sl in range(16):
                for a2 in range(4):
                    t = batch * 64 + sl * 4 + a2
                    out.append(te.matmul(
                        PS[4][a2 * 32:a2 * 32 + 16, sl * 16:sl * 16 + 16],
                        qa[:, t:t + 15 * T + 1:T],
                        ka[:, t:t + 15 * T + 1:T],
                        tile_position=(0, a2 * 32)))
            return out
        s.task("pe", sc, reads=[bqa, bka], writes=[bps[4]])
        s.task("dve", lambda v: v.tensor_add(
            gsc[:, 0:256], PS[4][:, 0:256], amask),
            reads=[bps[4], bcons], writes=[bgsc])
        s.task("dve", lambda v: v.reduce_max(
            gsc[:, 256:272],
            gsc[:, 0:256].rearrange("p (a b) -> p a b", b=16), AX.X),
            reads=[bgsc], writes=[bgsc])
        s.task("dve", lambda v: v.tensor_sub(
            gsc[:, 288:544].rearrange("p (a b) -> p a b", b=16),
            gsc[:, 0:256].rearrange("p (a b) -> p a b", b=16),
            gsc[:, 256:272].rearrange("p (a o) -> p a o", o=1).broadcast_to(
                (128, 16, 16))),
            reads=[bgsc], writes=[bgsc])
        s.task("act", lambda a: a.activation(
            gsc[:, 544:800], gsc[:, 288:544], AF.Exp, scale=INV_SQRT128),
            reads=[bgsc], writes=[bgsc])
        s.task("dve", lambda v: v.reduce_sum(
            gsc[:, 800:816],
            gsc[:, 544:800].rearrange("p (a b) -> p a b", b=16), AX.X),
            reads=[bgsc], writes=[bgsc])
        s.task("dve", lambda v: v.reciprocal(
            gsc[:, 816:832], gsc[:, 800:816]), reads=[bgsc], writes=[bgsc])
        s.task("dve", lambda v: v.tensor_mul(
            gsc[:, 832:1088].rearrange("p (a b) -> p a b", b=16),
            gsc[:, 544:800].rearrange("p (a b) -> p a b", b=16),
            gsc[:, 816:832].rearrange("p (a o) -> p a o", o=1).broadcast_to(
                (128, 16, 16))),
            reads=[bgsc], writes=[bgsc])
        s.task("dve", lambda v, batch=batch: v.reduce_sum(
            fsc[:, batch * 16:batch * 16 + 16],
            gsc[:, 832:1088].rearrange("p (a b) -> p a b", b=16), AX.X),
            reads=[bgsc], writes=[brs, bfsc])

    for a2 in range(4):
        def rs_dma(e, a2=a2):
            srcp = fsc[a2 * 32:a2 * 32 + 16, 0:256]
            dst = scr_rs[:, a2 * 256:(a2 + 1) * 256]
            return e.dma_start(dst, srcp)
        s.task("sp", rs_dma, reads=[brs], writes=[bscr["rs"]], inc=16)

    battn = s.buf("attn_all")
    for h in range(16):
        s.task("sp", lambda e, h=h: e.dma_start(
            rows[0:1, 0:T], scr_rs[h:h + 1, :]),
            reads=[bscr["rs"]], writes=[brow], inc=16)
        for th in range(2):
            psi = 4 + (h % 2)
            def bc(te, th=th, psi=psi):
                out = []
                pv = PS[psi][:, 0:512].rearrange(
                    "p (b s2 a) -> p b s2 a", b=8, s2=16, a=4)
                for a2 in range(4):
                    mv = rows[0:1, a2 * 256 + th * 128:a2 * 256 + th * 128 + 128]
                    out.append(te.matmul(
                        pv[:, :, :, a2], ones_f[0:1, 0:128], mv))
                return out
            s.task("pe", bc, reads=[brow, bcons], writes=[bps[psi]])
            s.task("dve", lambda v, h=h, th=th, psi=psi: v.tensor_mul(
                qa[:, h * T + th * 512:h * T + th * 512 + 512],
                fsc[:, 2 * T + th * 512:2 * T + th * 512 + 512],
                PS[psi][:, 0:512]),
                reads=[bps[psi], bvs], writes=[battn])

    s.task("sp", lambda e: e.dma_start(dbg[:, 4 * T:5 * T].rearrange(
        "p (a t2) -> p a t2", a=4)[0:16, :, 0:256],
        scr_rs.ap().rearrange("(hh h) t -> h hh t", hh=1)[:, 0, :].rearrange(
        "h (a t2) -> h a t2", a=4)[:, :, 0:256]) if False else e.dma_start(
        dbg[0:16, 4 * T:5 * T], scr_rs[:, :]),
        reads=[bscr["rs"]], writes=[bdbg], inc=16)
    s.task("pool", lambda e: e.dma_start(dbg[:, 5 * T:6 * T], qa[:, 0:T]),
        reads=[battn], writes=[bdbg], inc=16)

    def outa_cb(oc, th, ps, psb):
        t0 = th * 512
        s.task("dve", lambda v, oc=oc, t0=t0: v.scalar_tensor_tensor(
            x[:, oc * T + t0:oc * T + t0 + 512], ps, bias[:, 48 + oc:49 + oc],
            x[:, oc * T + t0:oc * T + t0 + 512], ALU.add, ALU.add),
            reads=[psb, bbias, bx[oc]], writes=[bx[oc]])

    proj_stream(woutp, D, D,
                lambda fc: (qa[:, fc * T:(fc + 1) * T], battn), outa_cb)

    s.task("dve", lambda v: v.memset(bias[:, 208:209], -8.0), writes=[bbias])

    bqh = s.buf("qh")
    sw_col = lambda m: bias[:, 176 + m:177 + m]
    invsw_col = lambda m: bias[:, 179 + m:180 + m]
    lnw_col = lambda oc: bias[:, 112 + oc:113 + oc]

    def weight_quant():
        pieces = []
        for mi, wS in enumerate((wgS, wuS, wdS)):
            nrow = 256 if mi < 2 else 1024
            ncol = I8 if mi < 2 else D
            for rc in range(nrow // 128):
                for pc in range(ncol // 2048):
                    pieces.append((mi, wS, rc, pc))
        assert len(pieces) == 24
        for pi, (mi, wS, rc, pc) in enumerate(pieces):
            fslc = fsc[:, 0:2048]
            s.task("sp", lambda e, wS=wS, rc=rc, pc=pc, fslc=fslc: e.dma_start(
                fslc, wS[rc * 128:(rc + 1) * 128, pc * 2048:(pc + 1) * 2048]),
                writes=[bfsc], inc=16)
            s.task("dve",
                   lambda v, pi=pi, fslc=fslc: v.scalar_tensor_tensor(
                       fslc, fslc, -1.0, fslc, ALU.mult, ALU.max,
                       accum_out=tmr[:, 104 + pi:105 + pi]),
                   reads=[bfsc], writes=[btmr, bfsc])
        s.task("pe", lambda te: te.matmul(
            PS[5][0:1, 0:24], ones_f[:, 0:1], tmr[:, 104:128]),
            reads=[btmr, bcons], writes=[bps[5]])
        s.task("dve", lambda v: v.tensor_copy(rows[0:1, 0:24], PS[5][0:1, 0:24]),
               reads=[bps[5]], writes=[brow])
        s.task("dve", lambda v: v.reduce_sum(
            rows[0:1, 32:35], rows[0:1, 0:24].rearrange(
                "p (m k) -> p m k", k=8), AX.X),
            reads=[brow], writes=[brow])
        s.task("sp", lambda e: e.dma_start(cc_min[0:1, 0:3], rows[0:1, 32:35]),
               reads=[brow], writes=[bscr["min"]], inc=16)
        s.task("pool", lambda g: g.collective_compute(
            "AllReduce", ALU.add, replica_groups=[list(range(8))],
            ins=[cc_min.ap().opt()], outs=[cc_mout.ap().opt()]),
            reads=[bscr["min"]], writes=[bscr["mout"]], inc=1, raw_inc=True)
        s.task("sp", lambda e: e.dma_start(rows[0:1, 36:39], cc_mout[0:1, 0:3]),
               reads=[bscr["mout"]], writes=[brow], inc=16)
        def mstats(v):
            v.tensor_scalar(rows[0:1, 40:43], rows[0:1, 36:39],
                            1.0 / (D * I8), 1e-5, ALU.mult, ALU.max)
            v.drain()
            return v.reciprocal(rows[0:1, 44:47], rows[0:1, 40:43])
        s.task("dve", mstats, reads=[brow], writes=[brow])
        for (roff, boff) in ((44, 176), (40, 179)):
            s.task("pe", lambda te, roff=roff: te.matmul(
                PS[5][:, 32:35], ones_f[0:1, 0:128], rows[0:1, roff:roff + 3]),
                reads=[brow, bcons], writes=[bps[5]])
            s.task("dve", lambda v, boff=boff: v.tensor_copy(
                bias[:, boff:boff + 3], PS[5][:, 32:35]),
                reads=[bps[5]], writes=[bbias])
        qjobs = []
        for mi, (wsrc, wdst, din, dout) in enumerate((
                (wgT, scr_qg, D, I8), (wuT, scr_qu, D, I8),
                (wdT, scr_qd, I8, D))):
            for fc in range(din // 128):
                for pc in range(dout // 1024):
                    qjobs.append((mi, wsrc, wdst, fc, pc))
        for qi, (mi, wsrc, wdst, fc, pc) in enumerate(qjobs):
            reg = qi % 2
            eng = "dve"
            eng2 = "pool" if qi % 2 else "dve"
            fslc = fsc[:, reg * 1024:reg * 1024 + 1024]
            qslc = qstage[:, reg * 1024:reg * 1024 + 1024]
            s.task("sp", lambda e, wsrc=wsrc, fc=fc, pc=pc, fslc=fslc:
                   e.dma_start(fslc, wsrc[fc * 128:(fc + 1) * 128,
                                          pc * 1024:(pc + 1) * 1024]),
                   writes=[bfsc], inc=16)
            s.task(eng, lambda v, fslc=fslc, mi=mi: v.tensor_scalar(
                fslc, fslc, sw_col(mi), 1.0, ALU.mult, ALU.min),
                reads=[bfsc, bbias], writes=[bfsc])
            s.task(eng2, lambda v, fslc=fslc: v.tensor_scalar(
                fslc, fslc, -1.0, MAGIC, ALU.max, ALU.add),
                reads=[bfsc], writes=[bfsc])
            s.task(eng2, lambda v, fslc=fslc, qslc=qslc: v.tensor_scalar_sub(
                qslc, fslc, MAGIC),
                reads=[bfsc], writes=[bgsc])
            s.task("sp", lambda e, wdst=wdst, fc=fc, pc=pc, qslc=qslc:
                   e.dma_start(wdst[fc * 128:(fc + 1) * 128,
                                    pc * 1024:(pc + 1) * 1024], qslc),
                   reads=[bgsc], writes=[bscr["qg"]], inc=16)

    def tm_chain(emits, reads, writes):
        def run(v):
            last = None
            for fn, dr in emits:
                if dr:
                    v.drain()
                last = fn(v)
            return last
        s.task("dve", run, reads=reads, writes=writes)

    def fold_partition_minmax(src_ap, src_buf, dst_col, op):
        for i in range(8):
            slot = (i % 2) * 128
            s.task("pe", lambda te, i=i, slot=slot: te.transpose(
                PS[5][:, slot:slot + 128],
                src_ap[:, i * 128:(i + 1) * 128], ident_f[:, 0:128]),
                reads=[src_buf, bcons], writes=[bps[5]])
            s.task("dve", lambda v, i=i, slot=slot, dst_col=dst_col:
                   v.tensor_reduce(
                       tmr[:, dst_col + i:dst_col + i + 1],
                       PS[5][:, slot:slot + 128], AX.X, op),
                   reads=[bps[5]], writes=[btmr])

    def ffn_sublayer():
        layernorm(emit_xln=False)
        s.task("pool", lambda v: v.tensor_copy(fsc[:, 0:T], x[:, 0:T]),
               reads=[bx[0]], writes=[bfsc])
        s.task("pool", lambda v: v.tensor_copy(fsc[:, T:2 * T], x[:, 0:T]),
               reads=[bx[0]], writes=[bfsc])
        for c in range(1, NF):
            s.task("dve", lambda v, c=c: v.tensor_tensor(
                fsc[:, 0:T], fsc[:, 0:T], x[:, c * T:(c + 1) * T], ALU.max),
                reads=[bfsc, bx[c]], writes=[bfsc])
            s.task("dve", lambda v, c=c: v.tensor_tensor(
                fsc[:, T:2 * T], fsc[:, T:2 * T], x[:, c * T:(c + 1) * T],
                ALU.min),
                reads=[bfsc, bx[c]], writes=[bfsc])
        fold_partition_minmax(fsc[:, 0:T], bfsc, 72, ALU.max)
        fold_partition_minmax(fsc[:, T:2 * T], bfsc, 80, ALU.min)
        def qmath(v):
            m = tmr[:, 16:24]; rstd = tmr[:, 56:64]
            amax = tmr[:, 72:80]; amin = tmr[:, 80:88]
            t1 = tmr[:, 88:96]; t2 = tmr[:, 96:104]
            A = tmr[:, 104:112]; cA = tmr[:, 112:120]
            v.tensor_sub(t1, amax, m)
            v.tensor_sub(t2, m, amin)
            v.drain()
            v.tensor_max(A, t1, t2)
            v.drain()
            v.tensor_mul(cA, A, rstd)
            v.drain()
            v.tensor_scalar_max(cA, cA, 1e-5)
            v.drain()
            v.reciprocal(t1, cA)
            v.tensor_scalar_mul(tmr[:, 120:128], cA, 1.0 / 127.0)
            v.drain()
            v.tensor_scalar_mul(t1, t1, 127.0)
            v.drain()
            v.tensor_mul(t2, t1, rstd)
            v.drain()
            v.scalar_tensor_tensor(A, m, -1.0, t2, ALU.mult, ALU.mult)
            v.drain()
            return v.tensor_scalar_add(A, A, MAGIC)
        s.task("dve", qmath, reads=[btmr], writes=[btmr])
        tm_to_row(96, 0, 8, 6)
        bcast_row(0, fsc[:, 0:T], bfsc, T, 6)
        tm_to_row(104, 0, 8, 7)
        bcast_row(0, fsc[:, T:2 * T], bfsc, T, 7)
        tm_to_row(120, 0, 8, 6)
        bcast_row(0, fsc[:, 2 * T:3 * T], bvs, T, 6)
        for c in range(NF):
            s.task("dve", lambda v, c=c: v.tensor_mul(
                gsc[:, 0:T], x[:, c * T:(c + 1) * T], fsc[:, 0:T]),
                reads=[bx[c], bfsc], writes=[bgsc])
            s.task("dve", lambda v, c=c: v.tensor_add(
                gsc[:, 0:T], gsc[:, 0:T], fsc[:, T:2 * T]),
                reads=[bgsc, bfsc], writes=[bgsc])
            s.task("dve", lambda v, c=c: v.tensor_scalar_sub(
                ka[:, c * T:(c + 1) * T], gsc[:, 0:T], MAGIC),
                reads=[bgsc], writes=[bqx])
        bsumsq = s.buf("sumsq_ps")
        bqst = s.buf("qstage_sq")
        for oc in range(64):
            wb = oc % 2
            s.task("sp", lambda e, oc=oc, wb=wb: e.dma_start(
                wstr[:, wb * 2048:wb * 2048 + 2048].rearrange(
                    "p (fc o) -> p fc o", o=128),
                scr_qg[:, oc * 128:(oc + 1) * 128].rearrange(
                    "(fc p) o -> p fc o", p=128)),
                reads=[bscr["qg"]], writes=[bwstr[wb]], inc=16)
            s.task("sp", lambda e, oc=oc, wb=wb: e.dma_start(
                wstr[:, 4096 + wb * 2048:4096 + wb * 2048 + 2048].rearrange(
                    "p (fc o) -> p fc o", o=128),
                scr_qu[:, oc * 128:(oc + 1) * 128].rearrange(
                    "(fc p) o -> p fc o", p=128)),
                reads=[bscr["qg"]], writes=[bwstr[2 + wb]], inc=16)
            for th in range(2):
                t0 = th * 512
                psg, psu = PS[th], PS[2 + th]
                for fc in range(NF):
                    s.task("pe", lambda te, fc=fc, wb=wb, t0=t0, psg=psg:
                           te.matmul(psg[:, 0:512],
                                     wstr[:, wb * 2048 + fc * 128:
                                          wb * 2048 + (fc + 1) * 128],
                                     ka[:, fc * T + t0:fc * T + t0 + 512],
                                     start=(fc == 0), stop=(fc == NF - 1)),
                           reads=[bwstr[wb], bqx], writes=[bps[th]])
                for fc in range(NF):
                    s.task("pe", lambda te, fc=fc, wb=wb, t0=t0, psu=psu:
                           te.matmul(psu[:, 0:512],
                                     wstr[:, 4096 + wb * 2048 + fc * 128:
                                          4096 + wb * 2048 + (fc + 1) * 128],
                                     ka[:, fc * T + t0:fc * T + t0 + 512],
                                     start=(fc == 0), stop=(fc == NF - 1)),
                           reads=[bwstr[2 + wb], bqx], writes=[bps[2 + th]])
                s.task("dve", lambda v, psg=psg, t0=t0: v.scalar_tensor_tensor(
                    gsc[:, 0:512], psg[:, 0:512], invsw_col(0),
                    fsc[:, 2 * T + t0:2 * T + t0 + 512], ALU.mult, ALU.mult),
                    reads=[bps[th], bbias, bvs], writes=[bgsc])
                s.task("act", lambda a: a.activation(
                    gsc[:, 512:1024], gsc[:, 0:512], AF.Silu),
                    reads=[bgsc], writes=[bgsc])
                s.task("dve", lambda v, psu=psu, t0=t0: v.scalar_tensor_tensor(
                    gsc[:, 1024:1536], psu[:, 0:512], invsw_col(1),
                    fsc[:, 2 * T + t0:2 * T + t0 + 512], ALU.mult, ALU.mult),
                    reads=[bps[2 + th], bbias, bvs], writes=[bgsc])
                s.task("pool", lambda v: v.tensor_mul(
                    gsc[:, 0:512], gsc[:, 512:1024], gsc[:, 1024:1536]),
                    reads=[bgsc], writes=[bgsc])
                s.task("sp", lambda e, oc=oc, t0=t0: e.dma_start(
                    scr_h[oc * 128:(oc + 1) * 128, t0:t0 + 512],
                    gsc[:, 0:512]),
                    reads=[bgsc], writes=[bscr["h"]], inc=16)
                if oc == 0:
                    s.task("dve", lambda v, t0=t0: v.tensor_single_scalar(
                        fsc[:, t0:t0 + 512], gsc[:, 0:512], lnw_col(0),
                        ALU.mult),
                        reads=[bgsc, bbias], writes=[bfsc])
                    s.task("dve", lambda v, t0=t0: v.tensor_single_scalar(
                        fsc[:, T + t0:T + t0 + 512], gsc[:, 0:512],
                        lnw_col(0), ALU.mult),
                        reads=[bgsc, bbias], writes=[bfsc])
                else:
                    s.task("dve", lambda v, oc=oc, t0=t0:
                           v.scalar_tensor_tensor(
                               fsc[:, t0:t0 + 512], gsc[:, 0:512],
                               lnw_col(oc), fsc[:, t0:t0 + 512],
                               ALU.mult, ALU.max),
                           reads=[bgsc, bbias, bfsc], writes=[bfsc])
                    s.task("dve", lambda v, oc=oc, t0=t0:
                           v.scalar_tensor_tensor(
                               fsc[:, T + t0:T + t0 + 512], gsc[:, 0:512],
                               lnw_col(oc), fsc[:, T + t0:T + t0 + 512],
                               ALU.mult, ALU.min),
                           reads=[bgsc, bbias, bfsc], writes=[bfsc])
                s.task("act", lambda a, t0=t0: a.activation(
                    qstage[:, 2048:2560], gsc[:, 0:512], AF.Square),
                    reads=[bgsc], writes=[bqst])
                s.task("pe", lambda te, oc=oc, th=th: te.matmul(
                    PS[4 + th][0:1, 0:512], ones_b[:, 0:1],
                    qstage[:, 2048:2560],
                    start=(oc == 0), stop=(oc == 63)),
                    reads=[bqst, bcons], writes=[bps[4 + th]])
        for th in range(2):
            s.task("dve", lambda v, th=th: v.tensor_copy(
                rows[0:1, th * 512:th * 512 + 512], PS[4 + th][0:1, 0:512]),
                reads=[bps[4 + th]], writes=[brow])
        row_to_tm(rows, 8, 8, [brow], 6)
        fold_partition_minmax(fsc[:, 0:T], bfsc, 72, ALU.max)
        fold_partition_minmax(fsc[:, T:2 * T], bfsc, 80, ALU.min)
        def midmath(v):
            zmax = tmr[:, 72:80]; zmin = tmr[:, 80:88]
            t1 = tmr[:, 88:96]; Ap = tmr[:, 96:104]
            var = tmr[:, 24:32]; lnv = tmr[:, 32:40]; rstdh = tmr[:, 40:48]
            cH = tmr[:, 112:120]
            v.tensor_scalar(var, tmr[:, 8:16], 1.0 / I8, 1e-6,
                            ALU.mult, ALU.add)
            v.tensor_scalar_mul(t1, zmin, -1.0)
            v.drain()
            return v.tensor_max(Ap, zmax, t1)
        s.task("dve", midmath, reads=[btmr], writes=[btmr])
        s.task("act", lambda a: a.activation(
            tmr[:, 32:40], tmr[:, 24:32], AF.Ln), reads=[btmr], writes=[btmr])
        s.task("act", lambda a: a.activation(
            tmr[:, 40:48], tmr[:, 32:40], AF.Exp, scale=-0.5),
            reads=[btmr], writes=[btmr])
        def midmath2(v):
            Ap = tmr[:, 96:104]; rstdh = tmr[:, 40:48]
            cH = tmr[:, 112:120]; t1 = tmr[:, 88:96]
            Qm = tmr[:, 96:104]; Dq2 = tmr[:, 104:112]
            v.tensor_mul(cH, Ap, rstdh)
            v.drain()
            v.tensor_scalar_max(cH, cH, 1e-5)
            v.drain()
            v.reciprocal(t1, cH)
            v.tensor_scalar_mul(Dq2, cH, 1.0 / 127.0)
            v.drain()
            v.tensor_scalar_mul(t1, t1, 127.0)
            v.drain()
            return v.tensor_mul(Qm, t1, rstdh)
        s.task("dve", midmath2, reads=[btmr], writes=[btmr])
        tm_to_row(96, 0, 8, 6)
        bcast_row(0, fsc[:, 0:T], bfsc, T, 6)
        tm_to_row(104, 0, 8, 7)
        bcast_row(0, fsc[:, T:2 * T], bvs, T, 7)
        spills = []
        for c in range(8):
            tsp = s.task("sp", lambda e, c=c: e.dma_start(
                scr_x[c * 128:(c + 1) * 128, :], x[:, c * T:(c + 1) * T]),
                reads=[bx[c]], writes=[bscr["x"]], inc=16)
            spills.append(tsp)
        for fc in range(64):
            s.task("sp", lambda e, fc=fc: e.dma_start(
                gsc[:, 0:T], scr_h[fc * 128:(fc + 1) * 128, :]),
                reads=[bscr["h"]], writes=[bgsc], inc=16)
            s.task("dve", lambda v, fc=fc: v.scalar_tensor_tensor(
                fsc[:, 2 * T:3 * T], gsc[:, 0:T], lnw_col(fc), fsc[:, 0:T],
                ALU.mult, ALU.mult),
                reads=[bgsc, bbias, bfsc], writes=[bvs])
            extra = ([bqa, bka, bqx, battn, bqb, bfe] + list(bxln)
                     if fc == 0 else [])
            xw = [bx[(fc - 48) // 2]] if fc >= 48 else []
            s.task("dve", lambda v, fc=fc: v.tensor_scalar(
                qh[:, fc * T:(fc + 1) * T], fsc[:, 2 * T:3 * T],
                MAGIC, MAGIC, ALU.add, ALU.subtract),
                reads=[bvs], writes=[bqh] + extra + xw,
                after=spills if fc >= 48 else ())
        for oc in range(16):
            for th in range(2):
                t0 = th * 512
                psd = PS[(oc % 2) * 2 + th]
                for q4 in range(4):
                    wb = q4 % 2
                    s.task("sp", lambda e, oc=oc, q4=q4, wb=wb: e.dma_start(
                        wstr[:, wb * 2048:wb * 2048 + 2048].rearrange(
                            "p (fc o) -> p fc o", o=128),
                        scr_qd[q4 * 2048:(q4 + 1) * 2048,
                               oc * 128:(oc + 1) * 128].rearrange(
                            "(fc p) o -> p fc o", p=128)),
                        reads=[bscr["qg"]], writes=[bwstr[wb]], inc=16)
                    for fc in range(16):
                        gfc = q4 * 16 + fc
                        s.task("pe", lambda te, fc=fc, wb=wb, gfc=gfc, t0=t0,
                               psd=psd, q4=q4: te.matmul(
                                   psd[:, 0:512],
                                   wstr[:, wb * 2048 + fc * 128:
                                        wb * 2048 + (fc + 1) * 128],
                                   qh[:, gfc * T + t0:gfc * T + t0 + 512],
                                   start=(gfc == 0), stop=(gfc == 63)),
                               reads=[bwstr[wb], bqh],
                               writes=[bps[(oc % 2) * 2 + th]])
                s.task("dve", lambda v, psd=psd, t0=t0: v.scalar_tensor_tensor(
                    gsc[:, 0:512], psd[:, 0:512], invsw_col(2),
                    fsc[:, T + t0:T + t0 + 512], ALU.mult, ALU.mult),
                    reads=[bps[(oc % 2) * 2 + th], bbias, bvs], writes=[bgsc])
                s.task("sp", lambda e, oc=oc, t0=t0: e.dma_start(
                    scr_h[oc * 128:(oc + 1) * 128, t0:t0 + 512],
                    gsc[:, 0:512]),
                    reads=[bgsc], writes=[bscr["h"]], inc=16)
        for c in range(NF):
            if c < 8:
                s.task("sp", lambda e, c=c: e.dma_start(
                    x[:, c * T:(c + 1) * T], scr_x[c * 128:(c + 1) * 128, :]),
                    reads=[bscr["x"]], writes=[bx[c], bqh], inc=16)
            s.task("sp", lambda e, c=c: e.dma_start(
                fsc[:, 0:T], scr_h[c * 128:(c + 1) * 128, :]),
                reads=[bscr["h"]], writes=[bfsc], inc=16)
            s.task("dve", lambda v, c=c: v.tensor_add(
                x[:, c * T:(c + 1) * T], x[:, c * T:(c + 1) * T],
                fsc[:, 0:T]),
                reads=[bfsc, bx[c]], writes=[bx[c]])

    def attn_b_sublayer():
        layernorm(emit_xln=True)
        bkv = [s.buf("kv%d" % i) for i in range(4)]
        kbuf = lambda wb: ka[:, 8192 + wb * 4096:8192 + wb * 4096 + 4096]
        vbuf = lambda wb: ka[:, wb * 4096:wb * 4096 + 4096]

        def q_out(oc, th, ps, psb):
            t0 = th * 512
            w = [bqb] + ([bqa, battn] if (oc == 0 and th == 0) else [])
            s.task("dve", lambda v, oc=oc, t0=t0: v.tensor_scalar_add(
                qa[:, oc * T + t0:oc * T + t0 + 512], ps,
                bias[:, 64 + oc:65 + oc]),
                reads=[psb, bbias], writes=w)
        proj_stream(wq_b, D, D,
                    lambda fc: (xln[:, fc * T:(fc + 1) * T], bxln[fc]), q_out)

        def k_out(oc, th, ps, psb):
            t0 = th * 512
            s.task("dve", lambda v, oc=oc, t0=t0: v.tensor_scalar_add(
                qstage[:, 2048:2560], ps, bias[:, 80 + oc:81 + oc]),
                reads=[psb, bbias], writes=[bqst2])
            s.task("sp", lambda e, oc=oc, t0=t0: e.dma_start(
                cc_kin[oc * 128:(oc + 1) * 128, t0:t0 + 512],
                qstage[:, 2048:2560]),
                reads=[bqst2], writes=[bscr["kin"]], inc=16)
        bqst2 = s.buf("qst2")
        proj_stream(wk_b, D, D,
                    lambda fc: (xln[:, fc * T:(fc + 1) * T], bxln[fc]), k_out)

        PSB6 = PD[3].ap().bitcast(BF16)
        vtstage = qstage[:, 0:512]
        bvts = s.buf("vts")

        def v_out(oc, th, ps, psb):
            t0 = th * 512
            s.task("dve", lambda v, oc=oc: v.tensor_scalar_add(
                qstage[:, 2048:2560], ps, bias[:, 192 + oc:193 + oc]),
                reads=[psb, bbias], writes=[bqst2])
            def vtr(te):
                out = []
                for q4 in range(4):
                    out.append(te.transpose(
                        PSB6[:, q4 * 128:(q4 + 1) * 128],
                        qstage[:, 2048 + q4 * 128:2048 + (q4 + 1) * 128],
                        ident_b[:]))
                return out
            s.task("pe", vtr, reads=[bqst2, bcons], writes=[bps[6], bps[7]])
            s.task("dve", lambda v: v.tensor_copy(
                vtstage[:, 0:512], PSB6[:, 0:512]),
                reads=[bps[6], bps[7]], writes=[bvts])
            s.task("sp", lambda e, oc=oc, t0=t0: e.dma_start(
                cc_vin[t0:t0 + 512, oc * 128:(oc + 1) * 128].rearrange(
                    "(q4 p) o -> p q4 o", p=128),
                vtstage[:, 0:512].rearrange("p (q4 o) -> p q4 o", o=128)),
                reads=[bvts], writes=[bscr["vin"]], inc=16)
        proj_stream(wv_b, D, D,
                    lambda fc: (xln[:, fc * T:(fc + 1) * T], bxln[fc]), v_out)

        groups = [[0, 1, 2, 3], [4, 5, 6, 7]]
        s.task("pool", lambda g: g.collective_compute(
            "AllGather", ALU.bypass, replica_groups=groups,
            ins=[cc_kin.ap().opt()], outs=[cc_kout.ap().opt()]),
            reads=[bscr["kin"]], writes=[bscr["kout"]], inc=1, raw_inc=True)
        s.task("pool", lambda g: g.collective_compute(
            "AllGather", ALU.bypass, replica_groups=groups,
            ins=[cc_vin.ap().opt()], outs=[cc_vout.ap().opt()]),
            reads=[bscr["vin"]], writes=[bscr["vout"]], inc=1, raw_inc=True)

        for pair in range(16):
            wb = pair % 2
            def kld(e, pair=pair, wb=wb):
                out = []
                for r in range(4):
                    out.append(e.dma_start(
                        kbuf(wb)[:, r * 1024:(r + 1) * 1024],
                        cc_kout[r * D + pair * 128:r * D + (pair + 1) * 128, :]))
                return out
            kl = s.task("sp", kld,
                        reads=[bscr["kout"]],
                        writes=[bkv[wb]] + ([bka, bqx] if pair == 0 else []),
                        inc=64)
            def vld(e, pair=pair, wb=wb):
                out = []
                for g8 in range(4):
                    src = cc_vout[g8 * 1024:(g8 + 1) * 1024,
                                  pair * 128:(pair + 1) * 128]
                    out.append(e.dma_start(
                        vbuf(wb)[:, g8 * 1024:(g8 + 1) * 1024].rearrange(
                            "p (kc j) -> p kc j", j=128),
                        src.rearrange("(kc p) j -> p kc j", p=128)))
                return out
            vl = s.task("sp", vld, reads=[bscr["vout"]],
                        writes=[bkv[2 + wb]], inc=64)
            for qhh in range(2):
                q0 = qhh * 512
                pvacc = PS[qhh]
                zacc = PS[2 + qhh]
                for chunk in range(32):
                    st = chunk * 2 + qhh
                    sb2 = 2 + ((chunk + qhh * 32) % 2)
                    psS = PD[sb2]
                    def sc(te, pair=pair, wb=wb, chunk=chunk, q0=q0, psS=psS):
                        a = te.matmul(
                            psS[:, 0:512],
                            kbuf(wb)[0:64, chunk * 128:(chunk + 1) * 128],
                            qa[0:64, pair * T + q0:pair * T + q0 + 512],
                            tile_position=(0, 0))
                        b = te.matmul(
                            psS[:, 512:1024],
                            kbuf(wb)[64:128, chunk * 128:(chunk + 1) * 128],
                            qa[64:128, pair * T + q0:pair * T + q0 + 512],
                            tile_position=(64, 0))
                        return [a, b]
                    s.task("pe", sc, reads=[bkv[wb], bqb],
                           writes=[bps[sb2 * 2], bps[sb2 * 2 + 1]])
                    pslot = qstage[:, (st % 2) * 1024:(st % 2) * 1024 + 1024]
                    expw = [bpsl[st % 2]]
                    if pair == 0 and qhh == 0 and chunk < 2:
                        expw = expw + [bvts, bqst2]
                    s.task("act", lambda a, psS=psS, pslot=pslot: a.activation(
                        pslot, psS[:, 0:1024], AF.Exp,
                        bias=bias[:, 208:209], scale=INV_SQRT32),
                        reads=[bps[sb2 * 2], bps[sb2 * 2 + 1], bbias],
                        writes=expw)
                    def pv(te, pair=pair, wb=wb, chunk=chunk, pslot=pslot,
                           pvacc=pvacc, zacc=zacc):
                        a = te.matmul(
                            pvacc[0:64, 0:512],
                            vbuf(wb)[:, chunk * 128:chunk * 128 + 64],
                            pslot[:, 0:512], tile_position=(0, 0),
                            start=(chunk == 0), stop=(chunk == 31))
                        b = te.matmul(
                            pvacc[64:128, 0:512],
                            vbuf(wb)[:, chunk * 128 + 64:chunk * 128 + 128],
                            pslot[:, 512:1024], tile_position=(0, 64),
                            start=(chunk == 0), stop=(chunk == 31))
                        za = te.matmul(
                            zacc[0:1, 0:512], ones_b[:, 0:1],
                            pslot[:, 0:512], tile_position=(0, 0),
                            start=(chunk == 0), stop=(chunk == 31))
                        zb = te.matmul(
                            zacc[32:33, 0:512], ones_b[:, 0:1],
                            pslot[:, 512:1024], tile_position=(0, 32),
                            start=(chunk == 0), stop=(chunk == 31))
                        return [a, b, za, zb]
                    s.task("pe", pv, reads=[bkv[2 + wb], bpsl[st % 2]],
                           writes=[bps[qhh], bps[2 + qhh]])
                s.task("dve", lambda v, zacc=zacc: v.reciprocal(
                    rows[0:1, 0:512], zacc[0:1, 0:512]),
                    reads=[bps[2 + qhh]], writes=[brow])
                s.task("dve", lambda v, zacc=zacc: v.reciprocal(
                    rows[32:33, 0:512], zacc[32:33, 0:512]),
                    reads=[bps[2 + qhh]], writes=[brow])
                def rzb(te):
                    a = te.matmul(PD[2][0:64, 0:512], ones_f[0:1, 0:64],
                                  rows[0:1, 0:512], tile_position=(0, 0))
                    b = te.matmul(PD[2][64:128, 0:512], ones_f[32:33, 0:64],
                                  rows[32:33, 0:512], tile_position=(32, 64))
                    return [a, b]
                s.task("pe", rzb, reads=[brow, bcons],
                       writes=[bps[4], bps[5]])
                s.task("dve", lambda v: v.tensor_copy(
                    gsc[:, 0:512], PD[2][:, 0:512]),
                    reads=[bps[4]], writes=[bgsc])
                fw = [bfe] + (list(bxln) if (pair == 0 and qhh == 0) else [])
                s.task("dve", lambda v, pair=pair, q0=q0, pvacc=pvacc:
                       v.tensor_mul(
                           xln[:, pair * T + q0:pair * T + q0 + 512],
                           pvacc[:, 0:512], gsc[:, 0:512]),
                       reads=[bps[qhh], bgsc], writes=fw)

        s.task("pool", lambda e: e.dma_start(dbg[:, 0 * T:1 * T], xln[:, 0:T]),
               reads=[bfe], writes=[bdbg], inc=16)
        s.task("pool", lambda e: e.dma_start(dbg[:, 1 * T:2 * T], qa[:, 0:T]),
               reads=[bqb], writes=[bdbg], inc=16)
        s.task("pool", lambda e: e.dma_start(
               dbg[:, 2 * T:3 * T], cc_kout[0:128, 0:1024]),
               reads=[bscr["kout"]], writes=[bdbg], inc=16)
        s.task("pool", lambda e: e.dma_start(
               dbg[:, 3 * T:4 * T].rearrange("p (a j) -> p a j", j=128),
               cc_vout[0:1024, 0:128].rearrange("(a p) j -> p a j", p=128)),
               reads=[bscr["vout"]], writes=[bdbg], inc=16)
        def outb_cb(oc, th, ps, psb):
            t0 = th * 512
            s.task("dve", lambda v, oc=oc, t0=t0: v.scalar_tensor_tensor(
                x[:, oc * T + t0:oc * T + t0 + 512], ps,
                bias[:, 96 + oc:97 + oc],
                x[:, oc * T + t0:oc * T + t0 + 512], ALU.add, ALU.add),
                reads=[psb, bbias, bx[oc]], writes=[bx[oc]])
        proj_stream(wo_b, D, D,
                    lambda fc: (xln[:, fc * T:(fc + 1) * T], bfe), outb_cb)

    bpsl = [s.buf("pslot%d" % i) for i in range(2)]

    if phases >= 2:
        weight_quant()
        ffn_sublayer()
    if phases >= 3:
        attn_b_sublayer()
    if phases >= 4:
        ffn_sublayer()

    for c in range(NF):
        s.task("sp", lambda e, c=c: e.dma_start(
            og[c * 128:(c + 1) * 128, :], x[:, c * T:(c + 1) * T]),
            reads=[bx[c]], inc=16)

    body_for = s.build_bodies(nc, paranoid=PARANOID[0])
    with nc.Block() as blk:
        blk.sync(body_for("sp"))
        blk.tensor(body_for("pe"))
        blk.scalar(body_for("act"))
        blk.vector(body_for("dve"))
        blk.gpsimd(body_for("pool"))
    return nc



def _consts_np():
    c = np.zeros((128, 512), np.float32)
    for a2 in range(4):
        for i in range(16):
            for j in range(16):
                if j > i:
                    c[a2 * 32 + i, np.arange(16) * 16 + j] = -1e9
    c[:, 256:384] = np.eye(128, dtype=np.float32)
    c[:, 384:512] = 1.0
    cb = np.zeros((128, 256), bfdt)
    cb[:, 0:128] = np.eye(128, dtype=np.float32).astype(bfdt)
    cb[:, 128:256] = np.float32(1.0)
    return c, cb


def kernel(L0_attn_a_x, L0_attn_a_Wqkv, L0_attn_a_bqkv, L0_attn_a_Wout,
           L0_attn_a_bout, L0_attn_b_Wq, L0_attn_b_bq, L0_attn_b_Wk,
           L0_attn_b_bk, L0_attn_b_Wv, L0_attn_b_bv, L0_attn_b_Wo,
           L0_attn_b_bo, L0_ffn_w_gate, L0_ffn_w_up, L0_ffn_w_down,
           L0_ffn_ln_weight, _phases=PHASES_DEFAULT):
    xfull = np.asarray(L0_attn_a_x, np.float32)
    cons, consb = _consts_np()

    def colT(vec, n):
        return np.ascontiguousarray(
            np.asarray(vec, np.float32).reshape(n, 128).T)

    bias = np.zeros((128, 256), np.float32)
    bias[:, 0:48] = colT(L0_attn_a_bqkv, 48)
    bias[:, 48:64] = colT(L0_attn_a_bout, 16)
    bias[:, 64:80] = colT(L0_attn_b_bq, 16)
    bias[:, 80:96] = colT(L0_attn_b_bk, 16)
    bias[:, 96:112] = colT(L0_attn_b_bo, 16)
    bias[:, 112:176] = colT(L0_ffn_ln_weight, 64)
    bias[:, 192:208] = colT(L0_attn_b_bv, 16)

    shared = {
        "wqkv": np.ascontiguousarray(
            np.asarray(L0_attn_a_Wqkv, np.float32).reshape(D, 6144)),
        "woutp": np.ascontiguousarray(
            np.asarray(L0_attn_a_Wout, np.float32).reshape(D, D)),
        "wq_b": np.ascontiguousarray(np.asarray(L0_attn_b_Wq, np.float32)),
        "wk_b": np.ascontiguousarray(np.asarray(L0_attn_b_Wk, np.float32)),
        "wv_b": np.ascontiguousarray(np.asarray(L0_attn_b_Wv, np.float32)),
        "wo_b": np.ascontiguousarray(np.asarray(L0_attn_b_Wo, np.float32)),
        "wgT": np.ascontiguousarray(np.asarray(L0_ffn_w_gate, np.float32).T),
        "wuT": np.ascontiguousarray(np.asarray(L0_ffn_w_up, np.float32).T),
        "wdT": np.ascontiguousarray(np.asarray(L0_ffn_w_down, np.float32).T),
        "biascols": bias,
        "bvrow": np.asarray(L0_attn_b_bv, np.float32).reshape(1, D).astype(bfdt),
        "consts": cons,
        "constb": consb,
    }
    wg = shared["wgT"]; wu = shared["wuT"]; wd = shared["wdT"]
    in_maps = []
    for core in range(8):
        b, sl = core // 4, core % 4
        m = dict(shared)
        m["xin"] = np.ascontiguousarray(xfull[b, sl * T:(sl + 1) * T, :].T)
        m["wgS"] = np.ascontiguousarray(wg[core * 256:(core + 1) * 256, :])
        m["wuS"] = np.ascontiguousarray(wu[core * 256:(core + 1) * 256, :])
        m["wdS"] = np.ascontiguousarray(wd[core * 1024:(core + 1) * 1024, :])
        in_maps.append(m)

    nc = build_nc(_phases)
    res = run_bass_kernel_spmd(nc, in_maps, core_ids=list(range(8)))
    global LAST_DBG
    LAST_DBG = [np.asarray(res.results[c].get("dbg")) for c in range(8)]
    out = np.zeros((2, SEQ, D), np.float32)
    for core in range(8):
        b, sl = core // 4, core % 4
        out[b, sl * T:(sl + 1) * T, :] = np.asarray(res.results[core]["og"]).T
    return out


# version 03 (final)
# speedup vs baseline: 1.2713x; 1.2713x over previous
import math
import numpy as np
import ml_dtypes

import concourse.bass as bass
import concourse.mybir as mybir
from concourse.bass_utils import run_bass_kernel_spmd

bfdt = ml_dtypes.bfloat16
F32 = mybir.dt.float32
BF16 = mybir.dt.bfloat16
AF = mybir.ActivationFunctionType
ALU = mybir.AluOpType
AX = mybir.AxisListType

MAGIC = float(np.float32(1.5 * 2 ** 23))
D = 2048
T = 1024
NF = 16
I8 = 8192
SEQ = 4096
INV_SQRT128 = 1.0 / math.sqrt(128.0)
INV_SQRT32 = 1.0 / math.sqrt(32.0)

PHASES_DEFAULT = 4
PARANOID = [True]
PARANOID_SEL = [None]
DRAIN_INC = [False]


ENGINES = ("sp", "pe", "act", "dve", "pool")


class Buf:
    __slots__ = ("name", "last_writer", "readers")

    def __init__(self, name):
        self.name = name
        self.last_writer = None
        self.readers = []


class Task:
    __slots__ = ("eng", "emit", "inc", "deps", "cum", "pre", "raw_inc",
                 "rd", "wr")


class Sched:
    def __init__(self):
        self.tasks = []
        self.latest = {e: None for e in ENGINES}

    def buf(self, name):
        return Buf(name)

    def task(self, eng, emit, reads=(), writes=(), inc=1, after=(),
             raw_inc=False):
        assert eng in ENGINES, eng
        t = Task()
        t.eng, t.emit, t.inc = eng, emit, inc
        t.raw_inc = raw_inc
        t.rd, t.wr = list(reads), list(writes)
        deps = set()
        for b in reads:
            if b.last_writer is not None:
                deps.add(b.last_writer)
        war_engines = set()
        for b in writes:
            if b.last_writer is not None:
                deps.add(b.last_writer)
            for r in b.readers:
                deps.add(r)
                if r.eng != eng:
                    war_engines.add(r.eng)
        for e2 in war_engines:
            if self.latest[e2] is not None:
                deps.add(self.latest[e2])
        for d in after:
            if d is not None:
                deps.add(d)
        t.deps = [d for d in deps if d is not t]
        self.tasks.append(t)
        self.latest[eng] = t
        for b in reads:
            b.readers.append(t)
        for b in writes:
            b.last_writer = t
            b.readers = []
        return t

    def build_bodies(self, nc, paranoid=False):
        sems = {e: nc.alloc_semaphore("sem_" + e) for e in ENGINES}
        cum = {e: 0 for e in ENGINES}
        for t in self.tasks:
            t.pre = dict(cum)
            cum[t.eng] += t.inc
            t.cum = cum[t.eng]
        total = dict(cum)
        streams = {e: [t for t in self.tasks if t.eng == e] for e in ENGINES}
        if paranoid:
            class _D:
                pass
            sel = PARANOID_SEL[0]
            for ti, t in enumerate(self.tasks):
                if sel is not None and not sel(ti, t):
                    continue
                nd = []
                for e in ENGINES:
                    if e != t.eng and t.pre[e] > 0:
                        d = _D(); d.eng = e; d.cum = t.pre[e]
                        nd.append(d)
                keep = [d for d in t.deps if d.eng == t.eng]
                t.deps = nd + keep

        def body_for(ename, _par=paranoid):
            hazard = ename in ("dve", "act", "pool")
            def body(eng):
                seen = {e: 0 for e in ENGINES}
                for t in streams[ename]:
                    need = {}
                    same = False
                    for d in t.deps:
                        if d.eng != ename:
                            need[d.eng] = max(need.get(d.eng, 0), d.cum)
                        else:
                            same = True
                    for e2 in sorted(need):
                        if seen[e2] < need[e2]:
                            eng.wait_ge(sems[e2], need[e2])
                            seen[e2] = need[e2]
                    if hazard and (same or _par):
                        eng.drain()
                    insts = t.emit(eng)
                    if isinstance(insts, (list, tuple)) and t.inc == 16 * len(insts):
                        for i2 in insts:
                            i2.then_inc(sems[t.eng], 16)
                    else:
                        last = (insts[-1] if isinstance(insts, (list, tuple))
                                else insts)
                        if DRAIN_INC[0] and hazard and t.inc == 1 and not t.raw_inc:
                            last = eng.drain()
                        last.then_inc(sems[t.eng], t.inc)
                if ename == "sp":
                    for e2 in ENGINES:
                        eng.wait_ge(sems[e2], total[e2])
            return body
        return body_for



def build_nc(phases=PHASES_DEFAULT):
    nc = bass.Bass("TRN2", target_bir_lowering=False)
    s = Sched()

    def di(name, shape, dt=F32):
        return nc.dram_tensor(name, shape, dt, kind="ExternalInput")

    xin = di("xin", [D, T])
    wqkv = di("wqkv", [D, 6144])
    woutp = di("woutp", [D, D])
    wq_b = di("wq_b", [D, D]); wk_b = di("wk_b", [D, D])
    wv_b = di("wv_b", [D, D]); wo_b = di("wo_b", [D, D])
    wgT = di("wgT", [D, I8]); wuT = di("wuT", [D, I8]); wdT = di("wdT", [I8, D])
    biascols = di("biascols", [128, 256])
    wgS = di("wgS", [256, I8]); wuS = di("wuS", [256, I8])
    wdS = di("wdS", [1024, D])
    bvrow = di("bvrow", [1, D], BF16)
    consts = di("consts", [128, 512])
    constb = di("constb", [128, 256], BF16)

    og = nc.dram_tensor("og", [D, T], F32, kind="ExternalOutput")
    dbg = nc.dram_tensor("dbg", [128, 8 * T], F32, kind="ExternalOutput")

    scr_rs = nc.dram_tensor("scr_rs", [16, T], F32, kind="Internal")
    scr_h = nc.dram_tensor("scr_h", [I8, T], F32, kind="Internal")
    scr_qg = nc.dram_tensor("scr_qg", [D, I8], BF16, kind="Internal")
    scr_qu = nc.dram_tensor("scr_qu", [D, I8], BF16, kind="Internal")
    scr_qd = nc.dram_tensor("scr_qd", [I8, D], BF16, kind="Internal")
    scr_x = nc.dram_tensor("scr_x", [D, T], F32, kind="Internal")
    cc_kin = nc.dram_tensor("cc_kin", [D, T], BF16, kind="Internal")
    cc_kout = nc.dram_tensor("cc_kout", [4 * D, T], BF16, kind="Internal")
    cc_vin = nc.dram_tensor("cc_vin", [T, D], BF16, kind="Internal")
    cc_vout = nc.dram_tensor("cc_vout", [SEQ, D], BF16, kind="Internal")
    cc_min = nc.dram_tensor("cc_min", [1, 4], F32, kind="Internal")
    cc_mout = nc.dram_tensor("cc_mout", [1, 4], F32, kind="Internal")

    def at(name, shape, dt, off):
        return nc.alloc_sbuf_tensor_at(name, shape, dt, offset=off)

    B0 = 16384
    cons = at("cons", [128, 512], F32, B0 + 0)
    consbt = at("consbt", [128, 256], BF16, B0 + 2048)
    tmr = at("tmr", [128, 128], F32, B0 + 2560)
    bias = at("bias", [128, 256], F32, B0 + 3072)
    rows = at("rows", [128, 1024], F32, B0 + 4096)
    fsc = at("fsc", [128, 3072], F32, B0 + 8192)
    gsc = at("gsc", [128, 1536], F32, B0 + 20480)
    wstr = at("wstr", [128, 8192], BF16, B0 + 26624)
    regA = at("regA", [128, 16384], BF16, B0 + 43008)
    regB = at("regB", [128, 16384], BF16, B0 + 75776)
    regC = at("regC", [128, 16384], BF16, B0 + 108544)
    x = at("x", [128, 16384], F32, B0 + 141312)
    qstage = at("qstage", [128, 2944], BF16, B0 + 206848)
    qh = at("qh", [128, 65536], BF16, B0 + 43008)
    END = B0 + 212736
    assert END <= 229376 - 32

    xln = regA
    qa = regB
    ka = regC

    bx = [s.buf("x%d" % c) for c in range(NF)]
    bxln = [s.buf("xln%d" % c) for c in range(NF)]
    bqa = s.buf("qa")
    bka = s.buf("ka")
    bwstr = [s.buf("wstr%d" % i) for i in range(4)]
    bfsc = s.buf("fsc")
    bgsc = s.buf("gsc")
    bcons = s.buf("cons")
    btmr = s.buf("tmr")
    brow = s.buf("rows")
    bbias = s.buf("bias")
    bps = [s.buf("ps%d" % i) for i in range(8)]
    bvs = s.buf("vsum")
    bqx = s.buf("qx")
    bqb = s.buf("q_b")
    bfe = s.buf("feats")
    bscr = {k: s.buf("scr_" + k) for k in
            ("rs", "h", "qg", "qu", "qd", "x", "kin", "kout", "vin", "vout",
             "min", "mout")}

    PD = [nc.alloc_psum_tensor("pd%d" % i, [128, 1024], F32) for i in range(4)]
    PS = [PD[i // 2][:, (i % 2) * 512:(i % 2) * 512 + 512] for i in range(8)]

    amask = cons[:, 0:256]
    ident_f = cons[:, 256:384]
    ones_f = cons[:, 384:512]
    ident_b = consbt[:, 0:128]
    ones_b = consbt[:, 128:256]

    for c in range(NF):
        s.task("sp", lambda e, c=c: e.dma_start(
            x[:, c * T:(c + 1) * T], xin[c * 128:(c + 1) * 128, :]),
            writes=[bx[c]], inc=16)
    s.task("sp", lambda e: e.dma_start(cons[:], consts[:]), writes=[bcons], inc=16)
    s.task("sp", lambda e: e.dma_start(consbt[:], constb[:]), writes=[bcons], inc=16)
    s.task("sp", lambda e: e.dma_start(bias[:], biascols[:]), writes=[bbias], inc=16)

    def row_to_tm(row_ap, tm_col0, nblk, reads, psi):
        for i in range(nblk):
            s.task("pe", lambda te, i=i: te.matmul(
                PS[psi][:, 64 + i:65 + i],
                row_ap[0:1, i * 128:(i + 1) * 128], ones_f[0:1, 0:1]),
                reads=list(reads) + [bcons], writes=[bps[psi]])
        s.task("dve", lambda v: v.tensor_copy(
            tmr[:, tm_col0:tm_col0 + nblk], PS[psi][:, 64:64 + nblk]),
            reads=[bps[psi]], writes=[btmr])

    def tm_to_row(tm_col0, row_off, nblk, psi):
        for i in range(nblk):
            s.task("pe", lambda te, i=i: te.matmul(
                PS[psi][0:1, 0:128],
                tmr[:, tm_col0 + i:tm_col0 + i + 1], ident_f[:, 0:128]),
                reads=[btmr, bcons], writes=[bps[psi]])
            s.task("dve", lambda v, i=i: v.tensor_copy(
                rows[0:1, row_off + i * 128:row_off + (i + 1) * 128],
                PS[psi][0:1, 0:128]),
                reads=[bps[psi]], writes=[brow])

    def bcast_row(row_off, dst_ap, dst_buf, n, psi):
        for h0 in range(0, n, 512):
            w = min(512, n - h0)
            s.task("pe", lambda te, h0=h0, w=w: te.matmul(
                PS[psi][:, 0:w], ones_f[0:1, 0:128],
                rows[0:1, row_off + h0:row_off + h0 + w]),
                reads=[brow, bcons], writes=[bps[psi]])
            s.task("dve", lambda v, h0=h0, w=w: v.tensor_copy(
                dst_ap[:, h0:h0 + w], PS[psi][:, 0:w]),
                reads=[bps[psi]], writes=[dst_buf])

    def layernorm(emit_xln=True):
        for (psi, use_sq, rowoff) in ((6, False, 0), (7, True, 0)):
            for half in range(2):
                t0 = half * 512
                for c in range(NF):
                    if use_sq:
                        s.task("act", lambda a, c=c, t0=t0: a.activation(
                            gsc[:, 0:512], x[:, c * T + t0:c * T + t0 + 512],
                            AF.Square),
                            reads=[bx[c]], writes=[bgsc])
                        mv, mb = gsc[:, 0:512], bgsc
                    else:
                        mv, mb = x[:, c * T + t0:c * T + t0 + 512], bx[c]
                    s.task("pe", lambda te, c=c, mv=mv, psi=psi: te.matmul(
                        PS[psi][0:1, 0:512], ones_f[:, 0:1], mv,
                        start=(c == 0), stop=(c == NF - 1)),
                        reads=[mb, bcons], writes=[bps[psi]])
                s.task("dve", lambda v, t0=t0, psi=psi: v.tensor_copy(
                    rows[0:1, t0:t0 + 512], PS[psi][0:1, 0:512]),
                    reads=[bps[psi]], writes=[brow])
            row_to_tm(rows, 0 if not use_sq else 8, 8, [brow], psi)
        def stats(v):
            m = tmr[:, 16:24]; ex2 = tmr[:, 24:32]
            msq = tmr[:, 32:40]; var = tmr[:, 40:48]
            v.tensor_scalar_mul(m, tmr[:, 0:8], 1.0 / D)
            v.tensor_scalar(ex2, tmr[:, 8:16], 1.0 / D, 1e-5, ALU.mult, ALU.add)
            v.scalar_tensor_tensor(msq, tmr[:, 0:8], 1.0 / D, tmr[:, 0:8],
                                   ALU.mult, ALU.mult)
            v.drain()
            return v.scalar_tensor_tensor(var, msq, -1.0 / D, ex2,
                                          ALU.mult, ALU.add)
        s.task("dve", stats, reads=[btmr], writes=[btmr])
        s.task("act", lambda a: a.activation(
            tmr[:, 48:56], tmr[:, 40:48], AF.Ln), reads=[btmr], writes=[btmr])
        s.task("act", lambda a: a.activation(
            tmr[:, 56:64], tmr[:, 48:56], AF.Exp, scale=-0.5),
            reads=[btmr], writes=[btmr])
        s.task("dve", lambda v: v.scalar_tensor_tensor(
            tmr[:, 64:72], tmr[:, 16:24], -1.0, tmr[:, 56:64],
            ALU.mult, ALU.mult),
            reads=[btmr], writes=[btmr])
        if not emit_xln:
            return
        tm_to_row(56, 0, 8, 6)
        bcast_row(0, fsc[:, 0:T], bfsc, T, 6)
        tm_to_row(64, 0, 8, 7)
        bcast_row(0, fsc[:, T:2 * T], bfsc, T, 7)
        for c in range(NF):
            s.task("dve", lambda v, c=c: v.tensor_mul(
                gsc[:, 0:T], x[:, c * T:(c + 1) * T], fsc[:, 0:T]),
                reads=[bx[c], bfsc], writes=[bgsc])
            s.task("dve", lambda v, c=c: v.tensor_add(
                xln[:, c * T:(c + 1) * T], gsc[:, 0:T], fsc[:, T:2 * T]),
                reads=[bgsc, bfsc], writes=[bxln[c]])

    def proj_stream(w_dram, din, dout, moving, out_cb, oc_grp=2, cast=True,
                    nmov=512, wdt=None):
        nfc = din // 128
        gw = 128 * oc_grp
        ngrp = dout // gw
        nth = T // nmov
        assert nfc * gw <= 4096, "weight group exceeds stream buffer"
        for g in range(ngrp):
            wb = g % 2
            def ld(e, g=g, wb=wb):
                src = w_dram[:, g * gw:(g + 1) * gw].rearrange(
                    "(fc p) o -> p fc o", p=128)
                dst = wstr[:, wb * 4096:wb * 4096 + nfc * gw].rearrange(
                    "p (fc o) -> p fc o", o=gw)
                return e.dma_start(dst, src)
            s.task("pool" if cast else "sp", ld, writes=[bwstr[wb]], inc=16)
            for oci in range(oc_grp):
                oc = g * oc_grp + oci
                for th in range(nth):
                    psi = (oc * nth + th) % 4
                    for fc in range(nfc):
                        mv, mbuf = moving(fc)
                        s.task("pe", lambda te, fc=fc, oci=oci, th=th, psi=psi,
                               wb=wb, mv=mv, nfc=nfc, gw=gw: te.matmul(
                                   PS[psi][:, 0:nmov],
                                   wstr[:, wb * 4096 + fc * gw + oci * 128:
                                        wb * 4096 + fc * gw + oci * 128 + 128],
                                   mv[:, th * nmov:(th + 1) * nmov],
                                   start=(fc == 0), stop=(fc == nfc - 1)),
                               reads=[bwstr[wb], mbuf], writes=[bps[psi]])
                    out_cb(oc, th, PS[psi][:, 0:nmov], bps[psi])

    layernorm()

    def qkv_out(oc, th, ps, psb):
        h, kind = divmod(oc, 3)
        t0 = th * 512
        if kind == 0:
            s.task("dve", lambda v: v.tensor_scalar_add(
                qa[:, h * T + t0:h * T + t0 + 512], ps, bias[:, oc:oc + 1]),
                reads=[psb, bbias], writes=[bqa])
        elif kind == 1:
            s.task("dve", lambda v: v.tensor_scalar_add(
                ka[:, h * T + t0:h * T + t0 + 512], ps, bias[:, oc:oc + 1]),
                reads=[psb, bbias], writes=[bka])
        else:
            if h == 0:
                s.task("dve", lambda v: v.tensor_scalar_add(
                    fsc[:, 2 * T + t0:2 * T + t0 + 512], ps,
                    bias[:, oc:oc + 1]),
                    reads=[psb, bbias], writes=[bvs])
            else:
                s.task("dve", lambda v: v.scalar_tensor_tensor(
                    fsc[:, 2 * T + t0:2 * T + t0 + 512], ps,
                    bias[:, oc:oc + 1], fsc[:, 2 * T + t0:2 * T + t0 + 512],
                    ALU.add, ALU.add),
                    reads=[psb, bbias, bvs], writes=[bvs])

    proj_stream(wqkv, D, 6144,
                lambda fc: (xln[:, fc * T:(fc + 1) * T], bxln[fc]), qkv_out)

    s.task("dve", lambda v: v.memset(PS[4][:, 0:256], 0.0), writes=[bps[4]])
    brs = s.buf("rs_all")
    for batch in range(16):
        def sc(te, batch=batch):
            out = []
            for sl in range(16):
                for a2 in range(4):
                    t = batch * 64 + sl * 4 + a2
                    out.append(te.matmul(
                        PS[4][a2 * 32:a2 * 32 + 16, sl * 16:sl * 16 + 16],
                        qa[:, t:t + 15 * T + 1:T],
                        ka[:, t:t + 15 * T + 1:T],
                        tile_position=(0, a2 * 32)))
            return out
        s.task("pe", sc, reads=[bqa, bka], writes=[bps[4]])
        s.task("dve", lambda v: v.tensor_add(
            gsc[:, 0:256], PS[4][:, 0:256], amask),
            reads=[bps[4], bcons], writes=[bgsc])
        s.task("dve", lambda v: v.reduce_max(
            gsc[:, 256:272],
            gsc[:, 0:256].rearrange("p (a b) -> p a b", b=16), AX.X),
            reads=[bgsc], writes=[bgsc])
        s.task("dve", lambda v: v.tensor_sub(
            gsc[:, 288:544].rearrange("p (a b) -> p a b", b=16),
            gsc[:, 0:256].rearrange("p (a b) -> p a b", b=16),
            gsc[:, 256:272].rearrange("p (a o) -> p a o", o=1).broadcast_to(
                (128, 16, 16))),
            reads=[bgsc], writes=[bgsc])
        s.task("act", lambda a: a.activation(
            gsc[:, 544:800], gsc[:, 288:544], AF.Exp, scale=INV_SQRT128),
            reads=[bgsc], writes=[bgsc])
        s.task("dve", lambda v: v.reduce_sum(
            gsc[:, 800:816],
            gsc[:, 544:800].rearrange("p (a b) -> p a b", b=16), AX.X),
            reads=[bgsc], writes=[bgsc])
        s.task("dve", lambda v: v.reciprocal(
            gsc[:, 816:832], gsc[:, 800:816]), reads=[bgsc], writes=[bgsc])
        s.task("dve", lambda v: v.tensor_mul(
            gsc[:, 832:1088].rearrange("p (a b) -> p a b", b=16),
            gsc[:, 544:800].rearrange("p (a b) -> p a b", b=16),
            gsc[:, 816:832].rearrange("p (a o) -> p a o", o=1).broadcast_to(
                (128, 16, 16))),
            reads=[bgsc], writes=[bgsc])
        s.task("dve", lambda v, batch=batch: v.reduce_sum(
            fsc[:, batch * 16:batch * 16 + 16],
            gsc[:, 832:1088].rearrange("p (a b) -> p a b", b=16), AX.X),
            reads=[bgsc], writes=[brs, bfsc])

    for a2 in range(4):
        def rs_dma(e, a2=a2):
            srcp = fsc[a2 * 32:a2 * 32 + 16, 0:256]
            dst = scr_rs[:, a2 * 256:(a2 + 1) * 256]
            return e.dma_start(dst, srcp)
        s.task("sp", rs_dma, reads=[brs], writes=[bscr["rs"]], inc=16)

    battn = s.buf("attn_all")
    for h in range(16):
        s.task("sp", lambda e, h=h: e.dma_start(
            rows[0:1, 0:T], scr_rs[h:h + 1, :]),
            reads=[bscr["rs"]], writes=[brow], inc=16)
        for th in range(2):
            psi = 4 + (h % 2)
            def bc(te, th=th, psi=psi):
                out = []
                pv = PS[psi][:, 0:512].rearrange(
                    "p (b s2 a) -> p b s2 a", b=8, s2=16, a=4)
                for a2 in range(4):
                    mv = rows[0:1, a2 * 256 + th * 128:a2 * 256 + th * 128 + 128]
                    out.append(te.matmul(
                        pv[:, :, :, a2], ones_f[0:1, 0:128], mv))
                return out
            s.task("pe", bc, reads=[brow, bcons], writes=[bps[psi]])
            s.task("dve", lambda v, h=h, th=th, psi=psi: v.tensor_mul(
                qa[:, h * T + th * 512:h * T + th * 512 + 512],
                fsc[:, 2 * T + th * 512:2 * T + th * 512 + 512],
                PS[psi][:, 0:512]),
                reads=[bps[psi], bvs], writes=[battn])

    def outa_cb(oc, th, ps, psb):
        t0 = th * 512
        s.task("dve", lambda v, oc=oc, t0=t0: v.scalar_tensor_tensor(
            x[:, oc * T + t0:oc * T + t0 + 512], ps, bias[:, 48 + oc:49 + oc],
            x[:, oc * T + t0:oc * T + t0 + 512], ALU.add, ALU.add),
            reads=[psb, bbias, bx[oc]], writes=[bx[oc]])

    proj_stream(woutp, D, D,
                lambda fc: (qa[:, fc * T:(fc + 1) * T], battn), outa_cb)

    s.task("dve", lambda v: v.memset(bias[:, 208:209], -8.0), writes=[bbias])

    bqh = s.buf("qh")
    sw_col = lambda m: bias[:, 176 + m:177 + m]
    invsw_col = lambda m: bias[:, 179 + m:180 + m]
    lnw_col = lambda oc: bias[:, 112 + oc:113 + oc]

    def weight_quant():
        pieces = []
        for mi, wS in enumerate((wgS, wuS, wdS)):
            nrow = 256 if mi < 2 else 1024
            ncol = I8 if mi < 2 else D
            for rc in range(nrow // 128):
                for pc in range(ncol // 2048):
                    pieces.append((mi, wS, rc, pc))
        assert len(pieces) == 24
        for pi, (mi, wS, rc, pc) in enumerate(pieces):
            fslc = fsc[:, 0:2048]
            s.task("sp", lambda e, wS=wS, rc=rc, pc=pc, fslc=fslc: e.dma_start(
                fslc, wS[rc * 128:(rc + 1) * 128, pc * 2048:(pc + 1) * 2048]),
                writes=[bfsc], inc=16)
            s.task("dve",
                   lambda v, pi=pi, fslc=fslc: v.scalar_tensor_tensor(
                       fslc, fslc, -1.0, fslc, ALU.mult, ALU.max,
                       accum_out=tmr[:, 104 + pi:105 + pi]),
                   reads=[bfsc], writes=[btmr, bfsc])
        s.task("pe", lambda te: te.matmul(
            PS[5][0:1, 0:24], ones_f[:, 0:1], tmr[:, 104:128]),
            reads=[btmr, bcons], writes=[bps[5]])
        s.task("dve", lambda v: v.tensor_copy(rows[0:1, 0:24], PS[5][0:1, 0:24]),
               reads=[bps[5]], writes=[brow])
        s.task("dve", lambda v: v.reduce_sum(
            rows[0:1, 32:35], rows[0:1, 0:24].rearrange(
                "p (m k) -> p m k", k=8), AX.X),
            reads=[brow], writes=[brow])
        s.task("sp", lambda e: e.dma_start(cc_min[0:1, 0:3], rows[0:1, 32:35]),
               reads=[brow], writes=[bscr["min"]], inc=16)
        s.task("pool", lambda g: g.collective_compute(
            "AllReduce", ALU.add, replica_groups=[list(range(8))],
            ins=[cc_min.ap().opt()], outs=[cc_mout.ap().opt()]),
            reads=[bscr["min"]], writes=[bscr["mout"]], inc=1, raw_inc=True)
        s.task("sp", lambda e: e.dma_start(rows[0:1, 36:39], cc_mout[0:1, 0:3]),
               reads=[bscr["mout"]], writes=[brow], inc=16)
        def mstats(v):
            v.tensor_scalar(rows[0:1, 40:43], rows[0:1, 36:39],
                            1.0 / (D * I8), 1e-5, ALU.mult, ALU.max)
            v.drain()
            return v.reciprocal(rows[0:1, 44:47], rows[0:1, 40:43])
        s.task("dve", mstats, reads=[brow], writes=[brow])
        for (roff, boff) in ((44, 176), (40, 179)):
            s.task("pe", lambda te, roff=roff: te.matmul(
                PS[5][:, 32:35], ones_f[0:1, 0:128], rows[0:1, roff:roff + 3]),
                reads=[brow, bcons], writes=[bps[5]])
            s.task("dve", lambda v, boff=boff: v.tensor_copy(
                bias[:, boff:boff + 3], PS[5][:, 32:35]),
                reads=[bps[5]], writes=[bbias])
        qjobs = []
        for mi, (wsrc, wdst, din, dout) in enumerate((
                (wgT, scr_qg, D, I8), (wuT, scr_qu, D, I8),
                (wdT, scr_qd, I8, D))):
            for fc in range(din // 128):
                for pc in range(dout // 1024):
                    qjobs.append((mi, wsrc, wdst, fc, pc))
        for qi, (mi, wsrc, wdst, fc, pc) in enumerate(qjobs):
            reg = qi % 2
            eng = "dve"
            eng2 = "dve"
            fslc = fsc[:, reg * 1024:reg * 1024 + 1024]
            qslc = qstage[:, reg * 1024:reg * 1024 + 1024]
            s.task("sp", lambda e, wsrc=wsrc, fc=fc, pc=pc, fslc=fslc:
                   e.dma_start(fslc, wsrc[fc * 128:(fc + 1) * 128,
                                          pc * 1024:(pc + 1) * 1024]),
                   writes=[bfsc], inc=16)
            s.task(eng, lambda v, fslc=fslc, mi=mi: v.tensor_scalar(
                fslc, fslc, sw_col(mi), 1.0, ALU.mult, ALU.min),
                reads=[bfsc, bbias], writes=[bfsc])
            s.task(eng2, lambda v, fslc=fslc: v.tensor_scalar(
                fslc, fslc, -1.0, MAGIC, ALU.max, ALU.add),
                reads=[bfsc], writes=[bfsc])
            s.task(eng2, lambda v, fslc=fslc, qslc=qslc: v.tensor_scalar_sub(
                qslc, fslc, MAGIC),
                reads=[bfsc], writes=[bgsc])
            s.task("sp", lambda e, wdst=wdst, fc=fc, pc=pc, qslc=qslc:
                   e.dma_start(wdst[fc * 128:(fc + 1) * 128,
                                    pc * 1024:(pc + 1) * 1024], qslc),
                   reads=[bgsc], writes=[bscr["qg"]], inc=16)

    def tm_chain(emits, reads, writes):
        def run(v):
            last = None
            for fn, dr in emits:
                if dr:
                    v.drain()
                last = fn(v)
            return last
        s.task("dve", run, reads=reads, writes=writes)

    def fold_partition_minmax(src_ap, src_buf, dst_col, op):
        for i in range(8):
            slot = (i % 2) * 128
            s.task("pe", lambda te, i=i, slot=slot: te.transpose(
                PS[5][:, slot:slot + 128],
                src_ap[:, i * 128:(i + 1) * 128], ident_f[:, 0:128]),
                reads=[src_buf, bcons], writes=[bps[5]])
            s.task("dve", lambda v, i=i, slot=slot, dst_col=dst_col:
                   v.tensor_reduce(
                       tmr[:, dst_col + i:dst_col + i + 1],
                       PS[5][:, slot:slot + 128], AX.X, op),
                   reads=[bps[5]], writes=[btmr])

    def ffn_sublayer():
        layernorm(emit_xln=False)
        s.task("dve", lambda v: v.tensor_copy(fsc[:, 0:T], x[:, 0:T]),
               reads=[bx[0]], writes=[bfsc])
        s.task("dve", lambda v: v.tensor_copy(fsc[:, T:2 * T], x[:, 0:T]),
               reads=[bx[0]], writes=[bfsc])
        for c in range(1, NF):
            s.task("dve", lambda v, c=c: v.tensor_tensor(
                fsc[:, 0:T], fsc[:, 0:T], x[:, c * T:(c + 1) * T], ALU.max),
                reads=[bfsc, bx[c]], writes=[bfsc])
            s.task("dve", lambda v, c=c: v.tensor_tensor(
                fsc[:, T:2 * T], fsc[:, T:2 * T], x[:, c * T:(c + 1) * T],
                ALU.min),
                reads=[bfsc, bx[c]], writes=[bfsc])
        fold_partition_minmax(fsc[:, 0:T], bfsc, 72, ALU.max)
        fold_partition_minmax(fsc[:, T:2 * T], bfsc, 80, ALU.min)
        def qmath(v):
            m = tmr[:, 16:24]; rstd = tmr[:, 56:64]
            amax = tmr[:, 72:80]; amin = tmr[:, 80:88]
            t1 = tmr[:, 88:96]; t2 = tmr[:, 96:104]
            A = tmr[:, 104:112]; cA = tmr[:, 112:120]
            v.tensor_sub(t1, amax, m)
            v.tensor_sub(t2, m, amin)
            v.drain()
            v.tensor_max(A, t1, t2)
            v.drain()
            v.tensor_mul(cA, A, rstd)
            v.drain()
            v.tensor_scalar_max(cA, cA, 1e-5)
            v.drain()
            v.reciprocal(t1, cA)
            v.tensor_scalar_mul(tmr[:, 120:128], cA, 1.0 / 127.0)
            v.drain()
            v.tensor_scalar_mul(t1, t1, 127.0)
            v.drain()
            v.tensor_mul(t2, t1, rstd)
            v.drain()
            v.scalar_tensor_tensor(A, m, -1.0, t2, ALU.mult, ALU.mult)
            v.drain()
            return v.tensor_scalar_add(A, A, MAGIC)
        s.task("dve", qmath, reads=[btmr], writes=[btmr])
        tm_to_row(96, 0, 8, 6)
        bcast_row(0, fsc[:, 0:T], bfsc, T, 6)
        tm_to_row(104, 0, 8, 7)
        bcast_row(0, fsc[:, T:2 * T], bfsc, T, 7)
        tm_to_row(120, 0, 8, 6)
        bcast_row(0, fsc[:, 2 * T:3 * T], bvs, T, 6)
        for c in range(NF):
            s.task("dve", lambda v, c=c: v.tensor_mul(
                gsc[:, 0:T], x[:, c * T:(c + 1) * T], fsc[:, 0:T]),
                reads=[bx[c], bfsc], writes=[bgsc])
            s.task("dve", lambda v, c=c: v.tensor_add(
                gsc[:, 0:T], gsc[:, 0:T], fsc[:, T:2 * T]),
                reads=[bgsc, bfsc], writes=[bgsc])
            s.task("dve", lambda v, c=c: v.tensor_scalar_sub(
                ka[:, c * T:(c + 1) * T], gsc[:, 0:T], MAGIC),
                reads=[bgsc], writes=[bqx])
        bsumsq = s.buf("sumsq_ps")
        bqst = s.buf("qstage_sq")
        for oc in range(64):
            wb = oc % 2
            s.task("sp", lambda e, oc=oc, wb=wb: e.dma_start(
                wstr[:, wb * 2048:wb * 2048 + 2048].rearrange(
                    "p (fc o) -> p fc o", o=128),
                scr_qg[:, oc * 128:(oc + 1) * 128].rearrange(
                    "(fc p) o -> p fc o", p=128)),
                reads=[bscr["qg"]], writes=[bwstr[wb]], inc=16)
            s.task("sp", lambda e, oc=oc, wb=wb: e.dma_start(
                wstr[:, 4096 + wb * 2048:4096 + wb * 2048 + 2048].rearrange(
                    "p (fc o) -> p fc o", o=128),
                scr_qu[:, oc * 128:(oc + 1) * 128].rearrange(
                    "(fc p) o -> p fc o", p=128)),
                reads=[bscr["qg"]], writes=[bwstr[2 + wb]], inc=16)
            for th in range(2):
                t0 = th * 512
                psg, psu = PS[th], PS[2 + th]
                for fc in range(NF):
                    s.task("pe", lambda te, fc=fc, wb=wb, t0=t0, psg=psg:
                           te.matmul(psg[:, 0:512],
                                     wstr[:, wb * 2048 + fc * 128:
                                          wb * 2048 + (fc + 1) * 128],
                                     ka[:, fc * T + t0:fc * T + t0 + 512],
                                     start=(fc == 0), stop=(fc == NF - 1)),
                           reads=[bwstr[wb], bqx], writes=[bps[th]])
                for fc in range(NF):
                    s.task("pe", lambda te, fc=fc, wb=wb, t0=t0, psu=psu:
                           te.matmul(psu[:, 0:512],
                                     wstr[:, 4096 + wb * 2048 + fc * 128:
                                          4096 + wb * 2048 + (fc + 1) * 128],
                                     ka[:, fc * T + t0:fc * T + t0 + 512],
                                     start=(fc == 0), stop=(fc == NF - 1)),
                           reads=[bwstr[2 + wb], bqx], writes=[bps[2 + th]])
                s.task("dve", lambda v, psg=psg, t0=t0: v.scalar_tensor_tensor(
                    gsc[:, 0:512], psg[:, 0:512], invsw_col(0),
                    fsc[:, 2 * T + t0:2 * T + t0 + 512], ALU.mult, ALU.mult),
                    reads=[bps[th], bbias, bvs], writes=[bgsc])
                s.task("act", lambda a: a.activation(
                    gsc[:, 512:1024], gsc[:, 0:512], AF.Silu),
                    reads=[bgsc], writes=[bgsc])
                s.task("dve", lambda v, psu=psu, t0=t0: v.scalar_tensor_tensor(
                    gsc[:, 1024:1536], psu[:, 0:512], invsw_col(1),
                    fsc[:, 2 * T + t0:2 * T + t0 + 512], ALU.mult, ALU.mult),
                    reads=[bps[2 + th], bbias, bvs], writes=[bgsc])
                s.task("dve", lambda v: v.tensor_mul(
                    gsc[:, 0:512], gsc[:, 512:1024], gsc[:, 1024:1536]),
                    reads=[bgsc], writes=[bgsc])
                s.task("sp", lambda e, oc=oc, t0=t0: e.dma_start(
                    scr_h[oc * 128:(oc + 1) * 128, t0:t0 + 512],
                    gsc[:, 0:512]),
                    reads=[bgsc], writes=[bscr["h"]], inc=16)
                if oc == 0:
                    s.task("dve", lambda v, t0=t0: v.tensor_single_scalar(
                        fsc[:, t0:t0 + 512], gsc[:, 0:512], lnw_col(0),
                        ALU.mult),
                        reads=[bgsc, bbias], writes=[bfsc])
                    s.task("dve", lambda v, t0=t0: v.tensor_single_scalar(
                        fsc[:, T + t0:T + t0 + 512], gsc[:, 0:512],
                        lnw_col(0), ALU.mult),
                        reads=[bgsc, bbias], writes=[bfsc])
                else:
                    s.task("dve", lambda v, oc=oc, t0=t0:
                           v.scalar_tensor_tensor(
                               fsc[:, t0:t0 + 512], gsc[:, 0:512],
                               lnw_col(oc), fsc[:, t0:t0 + 512],
                               ALU.mult, ALU.max),
                           reads=[bgsc, bbias, bfsc], writes=[bfsc])
                    s.task("dve", lambda v, oc=oc, t0=t0:
                           v.scalar_tensor_tensor(
                               fsc[:, T + t0:T + t0 + 512], gsc[:, 0:512],
                               lnw_col(oc), fsc[:, T + t0:T + t0 + 512],
                               ALU.mult, ALU.min),
                           reads=[bgsc, bbias, bfsc], writes=[bfsc])
                s.task("act", lambda a, t0=t0: a.activation(
                    qstage[:, 2048:2560], gsc[:, 0:512], AF.Square),
                    reads=[bgsc], writes=[bqst])
                s.task("pe", lambda te, oc=oc, th=th: te.matmul(
                    PS[4 + th][0:1, 0:512], ones_b[:, 0:1],
                    qstage[:, 2048:2560],
                    start=(oc == 0), stop=(oc == 63)),
                    reads=[bqst, bcons], writes=[bps[4 + th]])
        for th in range(2):
            s.task("dve", lambda v, th=th: v.tensor_copy(
                rows[0:1, th * 512:th * 512 + 512], PS[4 + th][0:1, 0:512]),
                reads=[bps[4 + th]], writes=[brow])
        row_to_tm(rows, 8, 8, [brow], 6)
        fold_partition_minmax(fsc[:, 0:T], bfsc, 72, ALU.max)
        fold_partition_minmax(fsc[:, T:2 * T], bfsc, 80, ALU.min)
        def midmath(v):
            zmax = tmr[:, 72:80]; zmin = tmr[:, 80:88]
            t1 = tmr[:, 88:96]; Ap = tmr[:, 96:104]
            var = tmr[:, 24:32]; lnv = tmr[:, 32:40]; rstdh = tmr[:, 40:48]
            cH = tmr[:, 112:120]
            v.tensor_scalar(var, tmr[:, 8:16], 1.0 / I8, 1e-6,
                            ALU.mult, ALU.add)
            v.tensor_scalar_mul(t1, zmin, -1.0)
            v.drain()
            return v.tensor_max(Ap, zmax, t1)
        s.task("dve", midmath, reads=[btmr], writes=[btmr])
        s.task("act", lambda a: a.activation(
            tmr[:, 32:40], tmr[:, 24:32], AF.Ln), reads=[btmr], writes=[btmr])
        s.task("act", lambda a: a.activation(
            tmr[:, 40:48], tmr[:, 32:40], AF.Exp, scale=-0.5),
            reads=[btmr], writes=[btmr])
        def midmath2(v):
            Ap = tmr[:, 96:104]; rstdh = tmr[:, 40:48]
            cH = tmr[:, 112:120]; t1 = tmr[:, 88:96]
            Qm = tmr[:, 96:104]; Dq2 = tmr[:, 104:112]
            v.tensor_mul(cH, Ap, rstdh)
            v.drain()
            v.tensor_scalar_max(cH, cH, 1e-5)
            v.drain()
            v.reciprocal(t1, cH)
            v.tensor_scalar_mul(Dq2, cH, 1.0 / 127.0)
            v.drain()
            v.tensor_scalar_mul(t1, t1, 127.0)
            v.drain()
            return v.tensor_mul(Qm, t1, rstdh)
        s.task("dve", midmath2, reads=[btmr], writes=[btmr])
        tm_to_row(96, 0, 8, 6)
        bcast_row(0, fsc[:, 0:T], bfsc, T, 6)
        tm_to_row(104, 0, 8, 7)
        bcast_row(0, fsc[:, T:2 * T], bvs, T, 7)
        spills = []
        for c in range(8):
            tsp = s.task("sp", lambda e, c=c: e.dma_start(
                scr_x[c * 128:(c + 1) * 128, :], x[:, c * T:(c + 1) * T]),
                reads=[bx[c]], writes=[bscr["x"]], inc=16)
            spills.append(tsp)
        for fc in range(64):
            s.task("sp", lambda e, fc=fc: e.dma_start(
                gsc[:, 0:T], scr_h[fc * 128:(fc + 1) * 128, :]),
                reads=[bscr["h"]], writes=[bgsc], inc=16)
            s.task("dve", lambda v, fc=fc: v.scalar_tensor_tensor(
                fsc[:, 2 * T:3 * T], gsc[:, 0:T], lnw_col(fc), fsc[:, 0:T],
                ALU.mult, ALU.mult),
                reads=[bgsc, bbias, bfsc], writes=[bvs])
            extra = ([bqa, bka, bqx, battn, bqb, bfe] + list(bxln)
                     if fc == 0 else [])
            xw = [bx[(fc - 48) // 2]] if fc >= 48 else []
            s.task("dve", lambda v, fc=fc: v.tensor_scalar(
                qh[:, fc * T:(fc + 1) * T], fsc[:, 2 * T:3 * T],
                MAGIC, MAGIC, ALU.add, ALU.subtract),
                reads=[bvs], writes=[bqh] + extra + xw,
                after=spills if fc >= 48 else ())
        for oc in range(16):
            for th in range(2):
                t0 = th * 512
                psd = PS[(oc % 2) * 2 + th]
                for q4 in range(4):
                    wb = q4 % 2
                    s.task("sp", lambda e, oc=oc, q4=q4, wb=wb: e.dma_start(
                        wstr[:, wb * 2048:wb * 2048 + 2048].rearrange(
                            "p (fc o) -> p fc o", o=128),
                        scr_qd[q4 * 2048:(q4 + 1) * 2048,
                               oc * 128:(oc + 1) * 128].rearrange(
                            "(fc p) o -> p fc o", p=128)),
                        reads=[bscr["qg"]], writes=[bwstr[wb]], inc=16)
                    for fc in range(16):
                        gfc = q4 * 16 + fc
                        s.task("pe", lambda te, fc=fc, wb=wb, gfc=gfc, t0=t0,
                               psd=psd, q4=q4: te.matmul(
                                   psd[:, 0:512],
                                   wstr[:, wb * 2048 + fc * 128:
                                        wb * 2048 + (fc + 1) * 128],
                                   qh[:, gfc * T + t0:gfc * T + t0 + 512],
                                   start=(gfc == 0), stop=(gfc == 63)),
                               reads=[bwstr[wb], bqh],
                               writes=[bps[(oc % 2) * 2 + th]])
                s.task("dve", lambda v, psd=psd, t0=t0: v.scalar_tensor_tensor(
                    gsc[:, 0:512], psd[:, 0:512], invsw_col(2),
                    fsc[:, T + t0:T + t0 + 512], ALU.mult, ALU.mult),
                    reads=[bps[(oc % 2) * 2 + th], bbias, bvs], writes=[bgsc])
                s.task("sp", lambda e, oc=oc, t0=t0: e.dma_start(
                    scr_h[oc * 128:(oc + 1) * 128, t0:t0 + 512],
                    gsc[:, 0:512]),
                    reads=[bgsc], writes=[bscr["h"]], inc=16)
        for c in range(NF):
            if c < 8:
                s.task("sp", lambda e, c=c: e.dma_start(
                    x[:, c * T:(c + 1) * T], scr_x[c * 128:(c + 1) * 128, :]),
                    reads=[bscr["x"]], writes=[bx[c], bqh], inc=16)
            s.task("sp", lambda e, c=c: e.dma_start(
                fsc[:, 0:T], scr_h[c * 128:(c + 1) * 128, :]),
                reads=[bscr["h"]], writes=[bfsc], inc=16)
            s.task("dve", lambda v, c=c: v.tensor_add(
                x[:, c * T:(c + 1) * T], x[:, c * T:(c + 1) * T],
                fsc[:, 0:T]),
                reads=[bfsc, bx[c]], writes=[bx[c]])

    def attn_b_sublayer():
        layernorm(emit_xln=True)
        bkv = [s.buf("kv%d" % i) for i in range(4)]
        kbuf = lambda wb: ka[:, 8192 + wb * 4096:8192 + wb * 4096 + 4096]
        vbuf = lambda wb: ka[:, wb * 4096:wb * 4096 + 4096]

        def q_out(oc, th, ps, psb):
            t0 = th * 512
            w = [bqb] + ([bqa, battn] if (oc == 0 and th == 0) else [])
            s.task("dve", lambda v, oc=oc, t0=t0: v.tensor_scalar_add(
                qa[:, oc * T + t0:oc * T + t0 + 512], ps,
                bias[:, 64 + oc:65 + oc]),
                reads=[psb, bbias], writes=w)
        proj_stream(wq_b, D, D,
                    lambda fc: (xln[:, fc * T:(fc + 1) * T], bxln[fc]), q_out)

        def k_out(oc, th, ps, psb):
            t0 = th * 512
            s.task("dve", lambda v, oc=oc, t0=t0: v.tensor_scalar_add(
                qstage[:, 2048:2560], ps, bias[:, 80 + oc:81 + oc]),
                reads=[psb, bbias], writes=[bqst2])
            s.task("sp", lambda e, oc=oc, t0=t0: e.dma_start(
                cc_kin[oc * 128:(oc + 1) * 128, t0:t0 + 512],
                qstage[:, 2048:2560]),
                reads=[bqst2], writes=[bscr["kin"]], inc=16)
        bqst2 = s.buf("qst2")
        proj_stream(wk_b, D, D,
                    lambda fc: (xln[:, fc * T:(fc + 1) * T], bxln[fc]), k_out)

        PSB6 = PD[3].ap().bitcast(BF16)
        vtstage = qstage[:, 0:512]
        bvts = s.buf("vts")

        def v_out(oc, th, ps, psb):
            t0 = th * 512
            s.task("dve", lambda v, oc=oc: v.tensor_scalar_add(
                qstage[:, 2048:2560], ps, bias[:, 192 + oc:193 + oc]),
                reads=[psb, bbias], writes=[bqst2])
            def vtr(te):
                out = []
                for q4 in range(4):
                    out.append(te.transpose(
                        PSB6[:, q4 * 128:(q4 + 1) * 128],
                        qstage[:, 2048 + q4 * 128:2048 + (q4 + 1) * 128],
                        ident_b[:]))
                return out
            s.task("pe", vtr, reads=[bqst2, bcons], writes=[bps[6], bps[7]])
            s.task("dve", lambda v: v.tensor_copy(
                vtstage[:, 0:512], PSB6[:, 0:512]),
                reads=[bps[6], bps[7]], writes=[bvts])
            s.task("sp", lambda e, oc=oc, t0=t0: e.dma_start(
                cc_vin[t0:t0 + 512, oc * 128:(oc + 1) * 128].rearrange(
                    "(q4 p) o -> p q4 o", p=128),
                vtstage[:, 0:512].rearrange("p (q4 o) -> p q4 o", o=128)),
                reads=[bvts], writes=[bscr["vin"]], inc=16)
        proj_stream(wv_b, D, D,
                    lambda fc: (xln[:, fc * T:(fc + 1) * T], bxln[fc]), v_out)

        groups = [[0, 1, 2, 3], [4, 5, 6, 7]]
        s.task("pool", lambda g: g.collective_compute(
            "AllGather", ALU.bypass, replica_groups=groups,
            ins=[cc_kin.ap().opt()], outs=[cc_kout.ap().opt()]),
            reads=[bscr["kin"]], writes=[bscr["kout"]], inc=1, raw_inc=True)
        s.task("pool", lambda g: g.collective_compute(
            "AllGather", ALU.bypass, replica_groups=groups,
            ins=[cc_vin.ap().opt()], outs=[cc_vout.ap().opt()]),
            reads=[bscr["vin"]], writes=[bscr["vout"]], inc=1, raw_inc=True)

        for pair in range(16):
            wb = pair % 2
            def kld(e, pair=pair, wb=wb):
                out = []
                for r in range(4):
                    out.append(e.dma_start(
                        kbuf(wb)[:, r * 1024:(r + 1) * 1024],
                        cc_kout[r * D + pair * 128:r * D + (pair + 1) * 128, :]))
                return out
            kl = s.task("sp", kld,
                        reads=[bscr["kout"]],
                        writes=[bkv[wb]] + ([bka, bqx] if pair == 0 else []),
                        inc=64)
            def vld(e, pair=pair, wb=wb):
                out = []
                for g8 in range(4):
                    src = cc_vout[g8 * 1024:(g8 + 1) * 1024,
                                  pair * 128:(pair + 1) * 128]
                    out.append(e.dma_start(
                        vbuf(wb)[:, g8 * 1024:(g8 + 1) * 1024].rearrange(
                            "p (kc j) -> p kc j", j=128),
                        src.rearrange("(kc p) j -> p kc j", p=128)))
                return out
            vl = s.task("sp", vld, reads=[bscr["vout"]],
                        writes=[bkv[2 + wb]], inc=64)
            for qhh in range(2):
                q0 = qhh * 512
                pvacc = PS[qhh]
                zacc = PS[2 + qhh]
                for chunk in range(32):
                    st = chunk * 2 + qhh
                    sb2 = 2 + ((chunk + qhh * 32) % 2)
                    psS = PD[sb2]
                    def sc(te, pair=pair, wb=wb, chunk=chunk, q0=q0, psS=psS):
                        a = te.matmul(
                            psS[:, 0:512],
                            kbuf(wb)[0:64, chunk * 128:(chunk + 1) * 128],
                            qa[0:64, pair * T + q0:pair * T + q0 + 512],
                            tile_position=(0, 0))
                        b = te.matmul(
                            psS[:, 512:1024],
                            kbuf(wb)[64:128, chunk * 128:(chunk + 1) * 128],
                            qa[64:128, pair * T + q0:pair * T + q0 + 512],
                            tile_position=(64, 0))
                        return [a, b]
                    s.task("pe", sc, reads=[bkv[wb], bqb],
                           writes=[bps[sb2 * 2], bps[sb2 * 2 + 1]])
                    pslot = qstage[:, (st % 2) * 1024:(st % 2) * 1024 + 1024]
                    expw = [bpsl[st % 2]]
                    if pair == 0 and qhh == 0 and chunk < 2:
                        expw = expw + [bvts, bqst2]
                    s.task("act", lambda a, psS=psS, pslot=pslot: a.activation(
                        pslot, psS[:, 0:1024], AF.Exp,
                        bias=bias[:, 208:209], scale=INV_SQRT32),
                        reads=[bps[sb2 * 2], bps[sb2 * 2 + 1], bbias],
                        writes=expw)
                    def pv(te, pair=pair, wb=wb, chunk=chunk, pslot=pslot,
                           pvacc=pvacc, zacc=zacc):
                        a = te.matmul(
                            pvacc[0:64, 0:512],
                            vbuf(wb)[:, chunk * 128:chunk * 128 + 64],
                            pslot[:, 0:512], tile_position=(0, 0),
                            start=(chunk == 0), stop=(chunk == 31))
                        b = te.matmul(
                            pvacc[64:128, 0:512],
                            vbuf(wb)[:, chunk * 128 + 64:chunk * 128 + 128],
                            pslot[:, 512:1024], tile_position=(0, 64),
                            start=(chunk == 0), stop=(chunk == 31))
                        za = te.matmul(
                            zacc[0:1, 0:512], ones_b[:, 0:1],
                            pslot[:, 0:512], tile_position=(0, 0),
                            start=(chunk == 0), stop=(chunk == 31))
                        zb = te.matmul(
                            zacc[32:33, 0:512], ones_b[:, 0:1],
                            pslot[:, 512:1024], tile_position=(0, 32),
                            start=(chunk == 0), stop=(chunk == 31))
                        return [a, b, za, zb]
                    s.task("pe", pv, reads=[bkv[2 + wb], bpsl[st % 2]],
                           writes=[bps[qhh], bps[2 + qhh]])
                s.task("dve", lambda v, zacc=zacc: v.reciprocal(
                    rows[0:1, 0:512], zacc[0:1, 0:512]),
                    reads=[bps[2 + qhh]], writes=[brow])
                s.task("dve", lambda v, zacc=zacc: v.reciprocal(
                    rows[32:33, 0:512], zacc[32:33, 0:512]),
                    reads=[bps[2 + qhh]], writes=[brow])
                def rzb(te):
                    a = te.matmul(PD[2][0:64, 0:512], ones_f[0:1, 0:64],
                                  rows[0:1, 0:512], tile_position=(0, 0))
                    b = te.matmul(PD[2][64:128, 0:512], ones_f[32:33, 0:64],
                                  rows[32:33, 0:512], tile_position=(32, 64))
                    return [a, b]
                s.task("pe", rzb, reads=[brow, bcons],
                       writes=[bps[4], bps[5]])
                s.task("dve", lambda v: v.tensor_copy(
                    gsc[:, 0:512], PD[2][:, 0:512]),
                    reads=[bps[4]], writes=[bgsc])
                fw = [bfe] + (list(bxln) if (pair == 0 and qhh == 0) else [])
                s.task("dve", lambda v, pair=pair, q0=q0, pvacc=pvacc:
                       v.tensor_mul(
                           xln[:, pair * T + q0:pair * T + q0 + 512],
                           pvacc[:, 0:512], gsc[:, 0:512]),
                       reads=[bps[qhh], bgsc], writes=fw)

        def outb_cb(oc, th, ps, psb):
            t0 = th * 512
            s.task("dve", lambda v, oc=oc, t0=t0: v.scalar_tensor_tensor(
                x[:, oc * T + t0:oc * T + t0 + 512], ps,
                bias[:, 96 + oc:97 + oc],
                x[:, oc * T + t0:oc * T + t0 + 512], ALU.add, ALU.add),
                reads=[psb, bbias, bx[oc]], writes=[bx[oc]])
        proj_stream(wo_b, D, D,
                    lambda fc: (xln[:, fc * T:(fc + 1) * T], bfe), outb_cb)

    bpsl = [s.buf("pslot%d" % i) for i in range(2)]

    if phases >= 2:
        weight_quant()
        ffn_sublayer()
    if phases >= 3:
        attn_b_sublayer()
    if phases >= 4:
        ffn_sublayer()

    for c in range(NF):
        s.task("sp", lambda e, c=c: e.dma_start(
            og[c * 128:(c + 1) * 128, :], x[:, c * T:(c + 1) * T]),
            reads=[bx[c]], inc=16)

    body_for = s.build_bodies(nc, paranoid=PARANOID[0])
    with nc.Block() as blk:
        blk.sync(body_for("sp"))
        blk.tensor(body_for("pe"))
        blk.scalar(body_for("act"))
        blk.vector(body_for("dve"))
        blk.gpsimd(body_for("pool"))
    return nc



def _consts_np():
    c = np.zeros((128, 512), np.float32)
    for a2 in range(4):
        for i in range(16):
            for j in range(16):
                if j > i:
                    c[a2 * 32 + i, np.arange(16) * 16 + j] = -1e9
    c[:, 256:384] = np.eye(128, dtype=np.float32)
    c[:, 384:512] = 1.0
    cb = np.zeros((128, 256), bfdt)
    cb[:, 0:128] = np.eye(128, dtype=np.float32).astype(bfdt)
    cb[:, 128:256] = np.float32(1.0)
    return c, cb


def kernel(L0_attn_a_x, L0_attn_a_Wqkv, L0_attn_a_bqkv, L0_attn_a_Wout,
           L0_attn_a_bout, L0_attn_b_Wq, L0_attn_b_bq, L0_attn_b_Wk,
           L0_attn_b_bk, L0_attn_b_Wv, L0_attn_b_bv, L0_attn_b_Wo,
           L0_attn_b_bo, L0_ffn_w_gate, L0_ffn_w_up, L0_ffn_w_down,
           L0_ffn_ln_weight, _phases=PHASES_DEFAULT):
    xfull = np.asarray(L0_attn_a_x, np.float32)
    cons, consb = _consts_np()

    def colT(vec, n):
        return np.ascontiguousarray(
            np.asarray(vec, np.float32).reshape(n, 128).T)

    bias = np.zeros((128, 256), np.float32)
    bias[:, 0:48] = colT(L0_attn_a_bqkv, 48)
    bias[:, 48:64] = colT(L0_attn_a_bout, 16)
    bias[:, 64:80] = colT(L0_attn_b_bq, 16)
    bias[:, 80:96] = colT(L0_attn_b_bk, 16)
    bias[:, 96:112] = colT(L0_attn_b_bo, 16)
    bias[:, 112:176] = colT(L0_ffn_ln_weight, 64)
    bias[:, 192:208] = colT(L0_attn_b_bv, 16)

    shared = {
        "wqkv": np.ascontiguousarray(
            np.asarray(L0_attn_a_Wqkv, np.float32).reshape(D, 6144)),
        "woutp": np.ascontiguousarray(
            np.asarray(L0_attn_a_Wout, np.float32).reshape(D, D)),
        "wq_b": np.ascontiguousarray(np.asarray(L0_attn_b_Wq, np.float32)),
        "wk_b": np.ascontiguousarray(np.asarray(L0_attn_b_Wk, np.float32)),
        "wv_b": np.ascontiguousarray(np.asarray(L0_attn_b_Wv, np.float32)),
        "wo_b": np.ascontiguousarray(np.asarray(L0_attn_b_Wo, np.float32)),
        "wgT": np.ascontiguousarray(np.asarray(L0_ffn_w_gate, np.float32).T),
        "wuT": np.ascontiguousarray(np.asarray(L0_ffn_w_up, np.float32).T),
        "wdT": np.ascontiguousarray(np.asarray(L0_ffn_w_down, np.float32).T),
        "biascols": bias,
        "bvrow": np.asarray(L0_attn_b_bv, np.float32).reshape(1, D).astype(bfdt),
        "consts": cons,
        "constb": consb,
    }
    wg = shared["wgT"]; wu = shared["wuT"]; wd = shared["wdT"]
    in_maps = []
    for core in range(8):
        b, sl = core // 4, core % 4
        m = dict(shared)
        m["xin"] = np.ascontiguousarray(xfull[b, sl * T:(sl + 1) * T, :].T)
        m["wgS"] = np.ascontiguousarray(wg[core * 256:(core + 1) * 256, :])
        m["wuS"] = np.ascontiguousarray(wu[core * 256:(core + 1) * 256, :])
        m["wdS"] = np.ascontiguousarray(wd[core * 1024:(core + 1) * 1024, :])
        in_maps.append(m)

    nc = build_nc(_phases)
    res = run_bass_kernel_spmd(nc, in_maps, core_ids=list(range(8)))
    global LAST_DBG
    LAST_DBG = [np.asarray(res.results[c].get("dbg")) for c in range(8)]
    out = np.zeros((2, SEQ, D), np.float32)
    for core in range(8):
        b, sl = core // 4, core % 4
        out[b, sl * T:(sl + 1) * T, :] = np.asarray(res.results[core]["og"]).T
    return out
